# version 10
# baseline (speedup 1.0000x reference)
import numpy as np
from contextlib import ExitStack

import concourse.bass as bass
import concourse.mybir as mybir
from concourse.bass_utils import run_bass_kernel_spmd

F32 = mybir.dt.float32
BF16 = mybir.dt.bfloat16
AF = mybir.ActivationFunctionType
ALU = mybir.AluOpType
AX = mybir.AxisListType

T = 8192
C = 1024
NCT = 8
SUB = 256
SC = 1024
NSC = T // SC
SUBS = SC // SUB
SL = 512
NSL = SC // SL
NE = 16
TOPK = 8
ROUTE_SCALE = 2.826
LN_EPS = 1e-5
NMAT = 55
MOE_PAIRS = None
MOE_SHARED = True
RACE_CHECK = True


class KB:
    ENGS = ("tensor", "vector", "scalar", "gpsimd", "sync")

    def __init__(self, nc, ctx):
        self.nc = nc
        self.steps = []
        self.sem = {}
        self.cnt = {}
        self.lastw = {e: {} for e in self.ENGS}
        names = ["GPE", "GDVE", "GACT", "GPOOL", "GLD", "GST", "GGP", "P0S", "CBS", "X1S", "TRS", "L1S"]
        for sc in range(NSC):
            for s in ("PE", "DVE", "ACT", "POOL", "LD", "ST", "GP"):
                names.append(f"{s}{sc}")
        for n in names:
            self.sem[n] = ctx.enter_context(nc.semaphore(n))
            self.cnt[n] = 0
        self.cur = "G"

    def S(self, base):
        return (("G" + base) if self.cur == "G" else (base + self.cur))

    def set_epoch(self, sc):
        self.cur = "G" if sc is None else str(sc)

    def emit(self, engine, fn, waits=None, inc=None, amt=1):
        w = {}
        for t in (waits or ()):
            if t is None:
                continue
            s, v = t
            if v > 0 and self.lastw[engine].get(s, 0) < v:
                w[s] = max(w.get(s, 0), v)
        for s, v in w.items():
            self.lastw[engine][s] = v
        if inc is not None:
            s = inc if inc in self.sem else self.S(inc)
            self.cnt[s] += amt
            self.steps.append((engine, fn, w, (s, amt)))
            return (s, self.cnt[s])
        self.steps.append((engine, fn, w, None))
        return None

    def load(self, dst, src, waits=None):
        return self.emit("sync", lambda e: e.dma_start(out=dst, in_=src),
                         waits, inc="LD", amt=16)

    def loadT(self, dst, src, waits=None):
        return self.emit("sync", lambda e: e.dma_start(out=dst, in_=src, transpose=True),
                         waits, inc="LD", amt=16)

    def store(self, dst, src, waits=None):
        return self.emit("scalar", lambda e: e.dma_start(out=dst, in_=src),
                         waits, inc="ST", amt=16)

    def gdma(self, dst, src, waits=None):
        return self.emit("gpsimd", lambda e: e.dma_start(out=dst, in_=src),
                         waits, inc="GP", amt=16)

    def barrier(self):
        cts = [(s, self.cnt[s]) for s in self.sem if self.cnt[s] > 0]
        for e in self.ENGS:
            self.emit(e, lambda eng: eng.nop(), cts)

    def finalize(self, block):
        per = {e: [] for e in self.ENGS}
        for st in self.steps:
            per[st[0]].append(st)

        def mk(lst):
            def run(eng):
                for (_, fn, w, inc) in lst:
                    for s, v in w.items():
                        eng.wait_ge(self.sem[s], v)
                    ins = fn(eng)
                    if inc is not None:
                        ins.then_inc(self.sem[inc[0]], inc[1])
            return run
        block.tensor(mk(per["tensor"]))
        block.vector(mk(per["vector"]))
        block.scalar(mk(per["scalar"]))
        block.gpsimd(mk(per["gpsimd"]))
        block.sync(mk(per["sync"]))


class Arena:
    def __init__(self, nc, ctx, name, cols):
        self.t = ctx.enter_context(
            nc.sbuf_tensor(name, [128, cols], F32, is_transpose_dest=True))
        self.cols = cols
        self.base = 0

    def f32(self, n):
        assert self.base + n <= self.cols, f"arena overflow {self.base}+{n}>{self.cols}"
        v = self.t[:, self.base:self.base + n]
        self.base += n
        return v

    def bf16(self, n):
        assert n % 2 == 0
        v = self.f32(n // 2).bitcast(BF16)
        assert v.shape == (128, n), f"bitcast shape {v.shape}"
        return v

    def mark(self):
        return self.base

    def reset(self, m):
        self.base = m


def build_program(stage=2, tap=None):
    nc = bass.Bass("TRN2", target_bir_lowering=False, debug=False,
                   detect_race_conditions=RACE_CHECK)

    x_d = nc.dram_tensor("x", [T, C], F32, kind="ExternalInput").ap()
    y_d = nc.dram_tensor("y", [T, C], F32, kind="ExternalOutput").ap()
    p_mk = nc.dram_tensor("time_mix_k", [C], F32, kind="ExternalInput").ap()
    p_mv = nc.dram_tensor("time_mix_v", [C], F32, kind="ExternalInput").ap()
    p_mr = nc.dram_tensor("time_mix_r", [C], F32, kind="ExternalInput").ap()
    p_td = nc.dram_tensor("time_decay", [C], F32, kind="ExternalInput").ap()
    p_tf = nc.dram_tensor("time_first", [C], F32, kind="ExternalInput").ap()
    p_lw = nc.dram_tensor("ln_w", [C], F32, kind="ExternalInput").ap()
    p_lb = nc.dram_tensor("ln_b", [C], F32, kind="ExternalInput").ap()
    wk_d = nc.dram_tensor("Wk", [C, C], F32, kind="ExternalInput").ap()
    wv_d = nc.dram_tensor("Wv", [C, C], F32, kind="ExternalInput").ap()
    wr_d = nc.dram_tensor("Wr", [C, C], F32, kind="ExternalInput").ap()
    wo_d = nc.dram_tensor("Wo", [C, C], F32, kind="ExternalInput").ap()
    gw_d = nc.dram_tensor("gate_w", [NE, C], F32, kind="ExternalInput").ap()
    eb_d = nc.dram_tensor("expert_bias", [NE], F32, kind="ExternalInput").ap()
    wg_d = nc.dram_tensor("w_gate", [NE, C, C], F32, kind="ExternalInput").ap()
    wu_d = nc.dram_tensor("w_up", [NE, C, C], F32, kind="ExternalInput").ap()
    wd_d = nc.dram_tensor("w_down", [NE, C, C], F32, kind="ExternalInput").ap()
    sg_d = nc.dram_tensor("shared_gate_w", [C, C], F32, kind="ExternalInput").ap()
    su_d = nc.dram_tensor("shared_up_w", [C, C], F32, kind="ExternalInput").ap()
    sd_d = nc.dram_tensor("shared_down_w", [C, C], F32, kind="ExternalInput").ap()

    taps = {}
    if tap == "moe":
        taps["t_ln1"] = nc.dram_tensor("t_ln1", [NCT, 128, T], F32, kind="ExternalOutput").ap()
        taps["t_ymoe"] = nc.dram_tensor("t_ymoe", [NCT, 128, T], F32, kind="ExternalOutput").ap()
        taps["t_sco"] = nc.dram_tensor("t_sco", [NE, T], F32, kind="ExternalOutput").ap()
        taps["t_cmb"] = nc.dram_tensor("t_cmb", [NE, T], F32, kind="ExternalOutput").ap()
        taps["t_msk"] = nc.dram_tensor("t_msk", [128, (T // 128) * NE], F32, kind="ExternalOutput").ap()
        taps["t_hp"] = nc.dram_tensor("t_hp", [2, 128, NSL * NCT * SL], F32, kind="ExternalOutput").ap()
        taps["t_mt7"] = nc.dram_tensor("t_mt7", [8, 128, 8, 128], F32, kind="ExternalOutput").ap()
        taps["t_wdb"] = nc.dram_tensor("t_wdb", [128, 1024], F32, kind="ExternalOutput").ap()
        taps["t_dps"] = nc.dram_tensor("t_dps", [128, SL], F32, kind="ExternalOutput").ap()
        taps["t_mt9"] = nc.dram_tensor("t_mt9", [8, 128, 8, 128], F32, kind="ExternalOutput").ap()
    bfc = nc.dram_tensor("bfc", [NMAT, C, C], BF16, kind="Internal").ap()
    mt = nc.dram_tensor("mt", [NMAT, 8, 128, 8, 128], BF16, kind="Internal").ap()
    gw_bf = nc.dram_tensor("gw_bf", [NE, C], BF16, kind="Internal").ap()
    x1_dram = nc.dram_tensor("x1d", [NCT, 128, T], F32, kind="Internal").ap()
    ln1_dram = nc.dram_tensor("l1d", [NCT, 128, T], BF16, kind="Internal").ap()
    cdram = nc.dram_tensor("cdram", [2, NE, SL], F32, kind="Internal").ap()

    def mat_src(m):
        if m < 4:
            return (wk_d, wv_d, wr_d, wo_d)[m]
        if m < 7:
            return (sg_d, su_d, sd_d)[m - 4]
        e, j = divmod(m - 7, 3)
        return (wg_d, wu_d, wd_d)[j][e]

    ctx = ExitStack()
    with ctx:
        sbt = lambda n, s, d, **kw: ctx.enter_context(nc.sbuf_tensor(n, s, d, **kw))
        ln1 = sbt("ln1", [128, NCT * SC], BF16)
        decay_b = sbt("decay_b", [128, NCT * SUB], F32)
        ones_bf = sbt("ones_bf", [128, 128], BF16)
        idn = sbt("idn", [128, 128], F32)
        prm = sbt("prm", [128, 64], F32)
        ebias = sbt("ebias", [128, 1], F32)
        carryS = sbt("carryS", [128, NCT], F32)
        xtc = sbt("xtc", [128, NCT], F32)
        ptmp = sbt("ptmp", [128, 128], F32)
        gwT = sbt("gwT", [128, NCT * NE], BF16, is_transpose_dest=True)
        tbuf = [sbt(f"tbuf{i}", [128, 1024], BF16, is_transpose_dest=True) for i in range(2)]
        stats = sbt("stats", [128, 12 * SUB], F32)
        x2tm = [sbt(f"x2tm{i}", [128, 1024], F32) for i in range(2)]
        cfm = sbt("cfm", [128, SL], F32)
        sco = sbt("sco", [128, SL], F32)
        sel = sbt("sel", [128, SL], F32)

        STT = lambda i: stats[:, i * SUB:(i + 1) * SUB]
        (s_mu, s_e2, s_var, s_sd, s_rstd, s_msq, s_vi, s_si,
         s_mu2, s_rstd2, s_sp1, s_sp2) = range(12)

        AR = Arena(nc, ctx, "arena", 38912)
        shared0 = AR.mark()
        attnW = [AR.bf16(8192) for _ in range(4)]
        xtm = AR.f32(2048)
        xfm = AR.f32(2048)
        xbf = AR.bf16(2048)
        ln0 = AR.f32(2048)
        xt = AR.f32(NCT * (SUB + 1))
        kxb = AR.bf16(2048)
        vxb = AR.bf16(2048)
        rxb = AR.bf16(2048)
        Kf = AR.f32(2048)
        Vb = AR.bf16(2048)
        Rb = AR.bf16(2048)
        Sf = AR.f32(2048)
        xsq = rxb
        rkb = vxb
        attn_end = AR.mark()
        AR.reset(shared0)
        y_acc = AR.f32(NCT * SC)
        upg = [AR.bf16(1024) for _ in range(2)]
        upu = [AR.bf16(1024) for _ in range(2)]
        wdb = [AR.bf16(1024) for _ in range(2)]
        hp = [AR.bf16(NSL * NCT * SL) for _ in range(2)]
        sgb = [AR.f32(SL) for _ in range(2)]
        hub = [AR.f32(SL) for _ in range(2)]
        cb = [[AR.f32(SL) for _ in range(2)] for _ in range(2)]
        r_tm = AR.f32(64)
        l_tm = AR.f32(64)
        rnk = AR.f32(64)
        msk = AR.f32(64)
        den = AR.f32(8)
        x1r = AR.f32(NCT * SL)
        fin = AR.f32(NCT * SL)
        ybf = fin[:, 0:NCT * SL // 2].bitcast(BF16)
        ysq = fin[:, NCT * SL // 2:].bitcast(BF16)
        moe_end = AR.mark()
        assert max(attn_end, moe_end) <= AR.cols, (attn_end, moe_end)

        def hp_off(it, s):
            return (s * NCT + it) * SL

        PB = [ctx.enter_context(nc.psum_tensor(f"pb{i}", [128, 512], F32))
              for i in range(8)]

        kb = KB(nc, ctx)
        block = ctx.enter_context(nc.Block())
        E = kb.emit

        war = {}

        def use(name, tick):
            if tick is not None:
                war.setdefault(name, []).append(tick)

        def take(name):
            return war.pop(name, [])

        def puse(i, tick):
            use(f"PS{i}", tick)

        def ptake(i):
            return take(f"PS{i}")

        ct_s = lambda buf, ct: buf[:, ct * SUB:(ct + 1) * SUB]

        kb.set_epoch(None)
        t_ms0 = E("gpsimd", lambda e: e.memset(idn[:], 0.0), inc="POOL")
        E("gpsimd", lambda e: e.memset(ones_bf[:], 1.0))
        E("gpsimd", lambda e: e.memset(carryS[:], 0.0))
        t_idn = E("gpsimd", lambda e: e.affine_select(
            out=idn[:], in_=idn[:], compare_op=ALU.not_equal, fill=1.0,
            base=0, pattern=[[-1, 128]], channel_multiplier=1),
            waits=[t_ms0], inc="POOL")
        t_zero = E("gpsimd", lambda e: e.memset(xtc[:], 0.0), inc="POOL")
        for _ct in range(NCT):
            t_zero = E("gpsimd", lambda e, ct=_ct: e.memset(
                xt[:, ct * 257: ct * 257 + 1], 0.0), inc="POOL")
        t_ones = t_idn

        pv = [(p_mk, 0), (p_mv, 8), (p_mr, 16), (p_td, 56), (p_tf, 56),
              (p_lw, 40), (p_lb, 48)]
        tt = None
        t_actp = None
        for i, (pd, dst) in enumerate(pv):
            tl = kb.load(ptmp[0:8, :], pd.rearrange("(a b) -> a b", b=128),
                         waits=[tt] if tt else None)
            tt = E("tensor", lambda e: e.transpose(
                PB[7][0:128, 0:8], ptmp[0:8, :], idn[0:8, 0:8]),
                waits=[tl, t_idn] + ptake(7), inc="PE")
            tc = E("scalar", lambda e, dst=dst: e.copy(
                out=prm[:, dst:dst + 8], in_=PB[7][0:128, 0:8]),
                waits=[tt, t_actp], inc="ACT")
            puse(7, tc)
            if i == 3:
                t_actp = E("scalar", lambda e: e.activation(
                    out=prm[:, 24:32], in_=prm[:, 56:64], func=AF.Sigmoid),
                    waits=[tc], inc="ACT")
            elif i == 4:
                t_actp = E("scalar", lambda e: e.activation(
                    out=prm[:, 32:40], in_=prm[:, 56:64], func=AF.Exp),
                    waits=[tc], inc="ACT")
        t_prm = ("GACT", kb.cnt["GACT"])
        t_one = E("gpsimd", lambda e: e.memset(STT(s_sp1), 1.0), inc="POOL")
        t_decb = None
        for ct in range(NCT):
            t_decb = E("vector", lambda e, ct=ct: e.tensor_scalar_mul(
                decay_b[:, ct * SUB:(ct + 1) * SUB], STT(s_sp1),
                prm[:, 24 + ct:25 + ct]), waits=[t_prm, t_one],
                inc="DVE" if ct == NCT - 1 else None)
        t_eb = kb.load(ebias[0:NE, 0:1], eb_d.rearrange("(a b) -> a b", b=1))

        kb.set_epoch(0)
        t_cast = {}
        t_gw_cast = kb.gdma(gw_bf, gw_d)
        _prev_cast = [t_gw_cast]

        def emit_cast(m):
            t_cast[m] = kb.gdma(bfc[m], mat_src(m), waits=[_prev_cast[0]])
            _prev_cast[0] = t_cast[m]

        mat_ready = {}

        def phase0(m, to_sbuf=None):
            if m not in t_cast:
                emit_cast(m)
            if to_sbuf is not None:
                dest = to_sbuf
                cp_ticks = []
                for k in range(4):
                    t0_ = kb.emit("sync", lambda e, m=m, k=k: e.dma_start(
                        out=tbuf[0][:], in_=bfc[m][:, (2 * k) * 128:(2 * k + 1) * 128],
                        transpose=True),
                        waits=[t_cast[m]] + take("tbuf0"), inc="TRS", amt=16)
                    t1_ = kb.emit("sync", lambda e, m=m, k=k: e.dma_start(
                        out=tbuf[1][:], in_=bfc[m][:, (2 * k + 1) * 128:(2 * k + 2) * 128],
                        transpose=True),
                        waits=take("tbuf1"), inc="TRS", amt=16)
                    last = None
                    for b in range(2):
                        ic = 2 * k + b
                        for ot in range(8):
                            last = E("gpsimd", lambda e, b=b, ot=ot, ic=ic, dest=dest: e.tensor_copy(
                                out=dest[:, (ot * 8 + ic) * 128:(ot * 8 + ic + 1) * 128],
                                in_=tbuf[b][:, ot * 128:(ot + 1) * 128]),
                                waits=[t1_],
                                inc="POOL" if ot == 7 else None)
                        use(f"tbuf{b}", last)
                    cp_ticks.append(last)
                st = None
                for ot in range(8):
                    st = kb.emit("scalar", lambda e, ot=ot, dest=dest: e.dma_start(
                        out=mt[m][ot], in_=dest[:, ot * 1024:(ot + 1) * 1024]),
                        waits=cp_ticks, inc="P0S", amt=16)
                mat_ready[m] = st
                return cp_ticks[-1]
            st = None
            for k in range(4):
                t0_ = kb.emit("sync", lambda e, m=m, k=k: e.dma_start(
                    out=tbuf[0][:], in_=bfc[m][:, (2 * k) * 128:(2 * k + 1) * 128],
                    transpose=True),
                    waits=[t_cast[m]] + take("tbuf0"), inc="TRS", amt=16)
                t1_ = kb.emit("sync", lambda e, m=m, k=k: e.dma_start(
                    out=tbuf[1][:], in_=bfc[m][:, (2 * k + 1) * 128:(2 * k + 2) * 128],
                    transpose=True),
                    waits=take("tbuf1"), inc="TRS", amt=16)
                for b in range(2):
                    ic = 2 * k + b
                    for ot in range(8):
                        st = kb.emit("scalar", lambda e, m=m, ot=ot, ic=ic, b=b: e.dma_start(
                            out=mt[m][ot][:, ic, :],
                            in_=tbuf[b][:, ot * 128:(ot + 1) * 128]),
                            waits=[t1_], inc="P0S", amt=16)
                use("tbuf0", st)
                use("tbuf1", st)
            mat_ready[m] = st
            return st

        attnW_p0 = [phase0(m, to_sbuf=attnW[m]) for m in range(4)]
        attnw_tick = [attnW_p0[m] for m in range(4)]
        t_gwT = None
        for ic in range(8):
            t_gwT = kb.loadT(gwT[:, ic * NE:(ic + 1) * NE],
                             gw_bf[:, ic * 128:(ic + 1) * 128],
                             waits=[t_gw_cast])
        for ee in range(2):
            for jj in range(3):
                phase0(7 + 3 * ee + jj)

        prev = {"ln1": None, "y_done": None}

        def ln_stats(src_bf, src_sq, mu_slot, rstd_slot, t_cast_done, t_sq_done):
            tmmA = None
            for ct in range(NCT):
                tmmA = E("tensor", lambda e, ct=ct: e.matmul(
                    PB[0][:, 0:SUB], ones_bf[:], ct_s(src_bf, ct),
                    start=(ct == 0), stop=(ct == NCT - 1)),
                    waits=[t_cast_done, t_ones] + (ptake(0) if ct == 0 else []),
                    inc="PE" if ct == NCT - 1 else None)
            tmmB = None
            for ct in range(NCT):
                tmmB = E("tensor", lambda e, ct=ct: e.matmul(
                    PB[1][:, 0:SUB], ones_bf[:], ct_s(src_sq, ct),
                    start=(ct == 0), stop=(ct == NCT - 1)),
                    waits=[t_sq_done] + (ptake(1) if ct == 0 else []),
                    inc="PE" if ct == NCT - 1 else None)
            t_mu = E("scalar", lambda e, mu_slot=mu_slot: e.mul(
                STT(mu_slot), PB[0][:, 0:SUB], np.float32(1.0 / C)), waits=[tmmA], inc="ACT")
            puse(0, t_mu)
            t_e2 = E("scalar", lambda e: e.mul(
                STT(s_e2), PB[1][:, 0:SUB], np.float32(1.0 / C)), waits=[tmmB], inc="ACT")
            puse(1, t_e2)
            t_msq = E("gpsimd", lambda e, mu_slot=mu_slot: e.tensor_mul(
                STT(s_msq), STT(mu_slot), STT(mu_slot)), waits=[t_mu], inc="POOL")
            t_var = E("vector", lambda e: e.tensor_sub(
                STT(s_var), STT(s_e2), STT(s_msq)), waits=[t_e2, t_msq], inc="DVE")
            t_sd = E("scalar", lambda e: e.activation(
                out=STT(s_sd), in_=STT(s_var), func=AF.Sqrt, bias=np.float32(LN_EPS)),
                waits=[t_var], inc="ACT")
            t_rstd = E("vector", lambda e, rstd_slot=rstd_slot: e.reciprocal(
                STT(rstd_slot), STT(s_sd)), waits=[t_sd], inc="DVE")
            return t_mu, t_var, t_rstd, tmmA, tmmB

        def attn_sub(sc, j):
            t0 = j * SUB
            lts = []
            for tb in range(2):
                lts.append(kb.load(
                    xtm[:, tb * 1024:(tb + 1) * 1024],
                    x_d[t0 + tb * 128: t0 + (tb + 1) * 128, :],
                    waits=take("xtm") if tb == 0 else []))
            t_xfm = None
            xw = take("xfm")
            for tb in range(2):
                for ct in range(NCT):
                    pi = 6 + ((tb * NCT + ct) % 2)
                    tt = E("tensor", lambda e, tb=tb, ct=ct, pi=pi: e.transpose(
                        PB[pi][:, 0:128],
                        xtm[:, tb * 1024 + ct * 128: tb * 1024 + (ct + 1) * 128],
                        idn[:]),
                        waits=[lts[tb], t_idn] + ptake(pi), inc="PE")
                    use("xtm", tt)
                    t_xfm = E("scalar", lambda e, tb=tb, ct=ct, pi=pi: e.copy(
                        out=xfm[:, ct * SUB + tb * 128: ct * SUB + (tb + 1) * 128],
                        in_=PB[pi][:, 0:128]),
                        waits=[tt] + xw, inc="ACT")
                    xw = []
                    puse(pi, t_xfm)
            tcast = None
            cw = take("xbf")
            for ct in range(NCT):
                tcast = E("scalar", lambda e, ct=ct: e.copy(
                    out=ct_s(xbf, ct), in_=ct_s(xfm, ct)),
                    waits=[t_xfm] + cw, inc="ACT" if ct == NCT - 1 else None)
                cw = []
            tsq = None
            sw = take("xsq")
            for ct in range(NCT):
                tsq = E("scalar", lambda e, ct=ct: e.activation(
                    out=ct_s(xsq, ct), in_=ct_s(xbf, ct), func=AF.Square),
                    waits=sw, inc="ACT" if ct == NCT - 1 else None)
                sw = []
            t_mu, t_var, t_rstd, tmmA, tmmB = ln_stats(xbf, xsq, s_mu, s_rstd, tcast, tsq)
            use("xbf", tmmA)
            use("rxq", tmmB)
            tv1 = E("gpsimd", lambda e: e.tensor_mul(STT(s_sp1), STT(s_var), STT(s_rstd)),
                    waits=[t_rstd], inc="POOL")
            tv2 = E("gpsimd", lambda e: e.tensor_mul(STT(s_vi), STT(s_sp1), STT(s_rstd)),
                    waits=[tv1], inc="POOL")
            tsd2 = E("scalar", lambda e: e.activation(
                out=STT(s_sp2), in_=STT(s_vi), func=AF.Sqrt, bias=np.float32(LN_EPS)),
                waits=[tv2], inc="ACT")
            t_si = E("vector", lambda e: e.reciprocal(STT(s_si), STT(s_sp2)),
                     waits=[tsd2], inc="DVE")
            tl0 = None
            lw = take("ln0")
            for ct in range(NCT):
                tl0 = E("vector", lambda e, ct=ct: e.tensor_sub(
                    ct_s(ln0, ct), ct_s(xfm, ct), STT(s_mu)),
                    waits=[t_mu, t_xfm] + lw,
                    inc="DVE" if ct == NCT - 1 else None)
                lw = []
            tsubp = tl0
            for ct in range(NCT):
                tl0 = E("vector", lambda e, ct=ct: e.tensor_mul(
                    ct_s(ln0, ct), ct_s(ln0, ct), STT(s_rstd)),
                    waits=[t_rstd, tsubp], inc="DVE" if ct == NCT - 1 else None)
            use("xfm", tl0)
            txt = None
            tw = take("Sf")
            xtw = take("xt")
            for ct in range(NCT):
                tt1 = E("gpsimd", lambda e, ct=ct: e.tensor_mul(
                    ct_s(Sf, ct), ct_s(ln0, ct), STT(s_si)),
                    waits=[tl0, t_si] + tw, inc="POOL")
                tw = []
                txt = E("vector", lambda e, ct=ct: e.tensor_scalar(
                    xt[:, ct * 257 + 1: ct * 257 + 257], ct_s(Sf, ct),
                    prm[:, 40 + ct:41 + ct], prm[:, 48 + ct:49 + ct],
                    ALU.mult, ALU.add),
                    waits=[tt1, t_prm] + xtw,
                    inc="DVE" if ct == NCT - 1 else None)
                xtw = []
            use("Sf", txt)
            tdm = None
            kw = take("Kf")
            for ct in range(NCT):
                tdm = E("gpsimd", lambda e, ct=ct: e.tensor_sub(
                    ct_s(Kf, ct), xt[:, ct * 257 + 1: ct * 257 + 257],
                    xt[:, ct * 257: ct * 257 + 256]),
                    waits=[txt, t_zero] + kw, inc="POOL" if ct == NCT - 1 else None)
                kw = []
            tmix = {}
            for name, buf, poff in (("k", kxb, 0), ("v", vxb, 8), ("r", rxb, 16)):
                tm = None
                mw = take({"k": "kxb", "v": "vxq", "r": "rxq"}[name])
                for ct in range(NCT):
                    tm = E("vector", lambda e, ct=ct, buf=buf, poff=poff: e.scalar_tensor_tensor(
                        ct_s(buf, ct), ct_s(Kf, ct), prm[:, poff + ct: poff + ct + 1],
                        xt[:, ct * 257: ct * 257 + 256], ALU.mult, ALU.add),
                        waits=[tdm, t_prm] + mw,
                        inc="DVE" if ct == NCT - 1 else None)
                    mw = []
                tmix[name] = tm
            tcar = None
            for ct in range(NCT):
                tcar = E("vector", lambda e, ct=ct: e.tensor_copy(
                    out=xtc[:, ct:ct + 1], in_=xt[:, ct * 257 + 256: ct * 257 + 257]),
                    waits=[txt], inc="DVE" if ct == NCT - 1 else None)
            tcol0 = None
            for ct in range(NCT):
                tcol0 = E("vector", lambda e, ct=ct: e.tensor_copy(
                    out=xt[:, ct * 257: ct * 257 + 1], in_=xtc[:, ct:ct + 1]),
                    waits=[tcar, tmix["k"], tmix["v"], tmix["r"]],
                    inc="DVE" if ct == NCT - 1 else None)
            use("xt", tcol0)
            evac_t = {}
            for name, src, m, tag in (("k", kxb, 0, "K"), ("v", vxb, 1, "V"),
                                      ("r", rxb, 2, "R")):
                ew = {"K": take("Kf2"), "V": take("Vb"), "R": take("Rb")}[tag]
                for ot in range(NCT):
                    pi = 2 + ot % 4
                    tmm = None
                    for ic in range(NCT):
                        tmm = E("tensor", lambda e, ot=ot, ic=ic, m=m, src=src, pi=pi: e.matmul(
                            PB[pi][:, 0:SUB],
                            attnW[m][:, (ot * 8 + ic) * 128:(ot * 8 + ic + 1) * 128],
                            ct_s(src, ic), start=(ic == 0), stop=(ic == NCT - 1)),
                            waits=[tmix[name], attnw_tick[m]] + (ptake(pi) if ic == 0 else []),
                            inc="PE" if ic == NCT - 1 else None)
                    if tag == "K":
                        tev = E("scalar", lambda e, ot=ot, pi=pi: e.copy(
                            out=ct_s(Kf, ot), in_=PB[pi][:, 0:SUB]),
                            waits=[tmm, tmix["r"]] + ew, inc="ACT")
                    elif tag == "V":
                        tev = E("scalar", lambda e, ot=ot, pi=pi: e.copy(
                            out=ct_s(Vb, ot), in_=PB[pi][:, 0:SUB]),
                            waits=[tmm] + ew, inc="ACT")
                    else:
                        tev = E("scalar", lambda e, ot=ot, pi=pi: e.activation(
                            out=ct_s(Rb, ot), in_=PB[pi][:, 0:SUB], func=AF.Sigmoid),
                            waits=[tmm] + ew, inc="ACT")
                    ew = []
                    puse(pi, tev)
                    evac_t[(tag, ot)] = tev
                    if ot == NCT - 1:
                        use({"K": "kxb", "V": "vxq", "R": "rxq"}[tag], tmm)
            tscan = {}
            for ct in range(NCT):
                ts = E("vector", lambda e, ct=ct: e.tensor_tensor_scan(
                    ct_s(Sf, ct), ct_s(decay_b, ct), ct_s(Kf, ct),
                    carryS[:, ct:ct + 1], ALU.mult, ALU.add),
                    waits=[evac_t[("K", ct)], t_decb, txt], inc="DVE")
                tscan[ct] = E("vector", lambda e, ct=ct: e.tensor_copy(
                    out=carryS[:, ct:ct + 1],
                    in_=Sf[:, ct * SUB + SUB - 1: ct * SUB + SUB]),
                    waits=[ts], inc="DVE")
            trk = {}
            rw = take("vxq")
            tp1 = None
            for ct in range(NCT):
                tp1 = E("vector", lambda e, ct=ct: e.scalar_tensor_tensor(
                    ct_s(Sf, ct), ct_s(Sf, ct), prm[:, 32 + ct:33 + ct],
                    ct_s(Kf, ct), ALU.mult, ALU.add),
                    waits=[tscan[ct], t_prm],
                    inc="DVE" if ct == NCT - 1 else None)
            tp2 = None
            for ct in range(NCT):
                tp2 = E("vector", lambda e, ct=ct: e.tensor_mul(
                    ct_s(Sf, ct), ct_s(Sf, ct), ct_s(Vb, ct)),
                    waits=[evac_t[("V", ct)], tp1],
                    inc="DVE" if ct == NCT - 1 else None)
            for ct in range(NCT):
                t3 = E("vector", lambda e, ct=ct: e.tensor_mul(
                    ct_s(rkb, ct), ct_s(Sf, ct), ct_s(Rb, ct)),
                    waits=[evac_t[("R", ct)], tp2] + rw, inc="DVE")
                rw = []
                use("Kf2", t3)
                use("Vb", t3)
                use("Rb", t3)
                trk[ct] = t3
            tatt = {}
            for ot in range(NCT):
                pi = 2 + ot % 4
                tmm = None
                for ic in range(NCT):
                    tmm = E("tensor", lambda e, ot=ot, ic=ic, pi=pi: e.matmul(
                        PB[pi][:, 0:SUB],
                        attnW[3][:, (ot * 8 + ic) * 128:(ot * 8 + ic + 1) * 128],
                        ct_s(rkb, ic), start=(ic == 0), stop=(ic == NCT - 1)),
                        waits=[trk[ic], attnw_tick[3]] + (ptake(pi) if ic == 0 else []),
                        inc="PE" if ic == NCT - 1 else None)
                use("vxq", tmm)
                ta = E("vector", lambda e, ot=ot, pi=pi: e.tensor_add(
                    ct_s(ln0, ot), ct_s(ln0, ot), PB[pi][:, 0:SUB]),
                    waits=[tmm, tl0], inc="DVE")
                puse(pi, ta)
                use("ln0", ta)
                tatt[ot] = ta
            tc2 = None
            cw = take("xbf")
            for ct in range(NCT):
                tc2 = E("scalar", lambda e, ct=ct: e.copy(
                    out=ct_s(xbf, ct), in_=ct_s(ln0, ct)),
                    waits=[tatt[ct]] + cw, inc="ACT" if ct == NCT - 1 else None)
                cw = []
            ts2 = None
            sw = take("xsq")
            for ct in range(NCT):
                ts2 = E("scalar", lambda e, ct=ct: e.activation(
                    out=ct_s(xsq, ct), in_=ct_s(xbf, ct), func=AF.Square),
                    waits=sw, inc="ACT" if ct == NCT - 1 else None)
                sw = []
            t_mu2, t_var2, t_rstd2, tmmA, tmmB = ln_stats(xbf, xsq, s_mu2, s_rstd2, tc2, ts2)
            use("xbf", tmmA)
            use("rxq", tmmB)
            tx1 = {}
            tq1 = None
            for ct in range(NCT):
                tq1 = E("gpsimd", lambda e, ct=ct: e.tensor_sub(
                    xt[:, ct * 257 + 1: ct * 257 + 257], ct_s(ln0, ct), STT(s_mu2)),
                    waits=[t_mu2, tatt[ct], tcol0],
                    inc="POOL" if ct == NCT - 1 else None)
            tq2 = None
            for ct in range(NCT):
                tq2 = E("gpsimd", lambda e, ct=ct: e.tensor_mul(
                    xt[:, ct * 257 + 1: ct * 257 + 257],
                    xt[:, ct * 257 + 1: ct * 257 + 257], STT(s_rstd2)),
                    waits=[t_rstd2, tq1],
                    inc="POOL" if ct == NCT - 1 else None)
            for ct in range(NCT):
                tx1[ct] = E("gpsimd", lambda e, ct=ct: e.tensor_add(
                    ct_s(xfm, ct), xt[:, ct * 257 + 1: ct * 257 + 257],
                    ct_s(xfm, ct)),
                    waits=[tl0, tq2], inc="POOL")
                use("ln0", tx1[ct])
                use("xt", tx1[ct])
            if tap is None:
                for ct in range(NCT):
                    tst = kb.emit("scalar", lambda e, ct=ct, t0=t0: e.dma_start(
                        out=x1_dram[ct][:, t0:t0 + SUB], in_=ct_s(xfm, ct)),
                        waits=[tx1[ct]], inc="X1S", amt=16)
                    use("xfm", tst)
                prev["x1st"] = tst
            else:
                for ct in range(NCT):
                    if tap == "xt":
                        tw, view = [txt, tx1[ct]], xt[:, ct * 257 + 1: ct * 257 + 257]
                    elif tap == "K":
                        tw, view = [evac_t[("K", ct)], tscan[ct], tx1[ct]], ct_s(Kf, ct)
                    elif tap == "S":
                        tw, view = [trk[ct], tx1[ct]], ct_s(Sf, ct)
                    elif tap == "att":
                        tw, view = [tatt[ct], tx1[ct]], ct_s(ln0, ct)
                    else:
                        tw, view = [tx1[ct]], ct_s(xfm, ct)
                    tst = kb.store(x1_dram[ct][:, t0:t0 + SUB], view, waits=tw)
                    use("xfm", tst)
                    use("ln0", tst)
                    use("xt", tst)
                    use("Kf", tst)
                    use("Sf", tst)
            tc3 = None
            cw = take("xbf")
            for ct in range(NCT):
                tc3 = E("scalar", lambda e, ct=ct: e.copy(
                    out=ct_s(xbf, ct), in_=ct_s(xfm, ct)),
                    waits=[tx1[ct]] + cw, inc="ACT" if ct == NCT - 1 else None)
                cw = []
            ts3 = None
            sw = take("xsq")
            for ct in range(NCT):
                ts3 = E("scalar", lambda e, ct=ct: e.activation(
                    out=ct_s(xsq, ct), in_=ct_s(xbf, ct), func=AF.Square),
                    waits=sw, inc="ACT" if ct == NCT - 1 else None)
                sw = []
            t_mul, t_varl, t_rstdl, tmmA, tmmB = ln_stats(xbf, xsq, s_mu2, s_rstd2, tc3, ts3)
            use("xbf", tmmA)
            use("rxq", tmmB)
            off = (j % SUBS) * SUB
            tln1 = None
            kw = take("Kf") + take("Kf2")
            for ct in range(NCT):
                tln1 = E("vector", lambda e, ct=ct: e.tensor_sub(
                    ct_s(Kf, ct), ct_s(xfm, ct), STT(s_mu2)),
                    waits=[t_mul, tx1[ct]] + kw,
                    inc="DVE" if ct == NCT - 1 else None)
                kw = []
            tlsub = tln1
            lww = take("ln1w") if (j % SUBS) == 0 else []
            for ct in range(NCT):
                tln1 = E("vector", lambda e, ct=ct, off=off: e.tensor_mul(
                    ln1[:, ct * SC + off: ct * SC + off + SUB],
                    ct_s(Kf, ct), STT(s_rstd2)),
                    waits=[t_rstdl, tlsub] + lww, inc="DVE" if ct == NCT - 1 else None)
                lww = []
            use("Kf", tln1)
            use("xfm", tln1)
            prev["ln1"] = tln1
            if (j % SUBS) == SUBS - 1:
                tls = None
                for ct in range(NCT):
                    tls = kb.emit("scalar", lambda e, ct=ct, sc=sc: e.dma_start(
                        out=ln1_dram[ct][:, sc * SC:(sc + 1) * SC],
                        in_=ln1[:, ct * SC:(ct + 1) * SC]),
                        waits=[tln1], inc="L1S", amt=16)
                    use("ln1w", tls)
                prev["l1st"] = tls

        def load_slab(m, ot, dstview, warkey):
            return kb.load(dstview, mt[m][ot], waits=[mat_ready[m]] + take(warkey))

        def moe_sc(sc):
            tl1 = None
            l1w = take("ln1buf")
            for ct in range(NCT):
                tl1 = kb.load(ln1[:, ct * SC:(ct + 1) * SC],
                              ln1_dram[ct][:, sc * SC:(sc + 1) * SC],
                              waits=[prev.get("l1st")] + l1w)
                l1w = []
            prev["ln1"] = tl1
            if tap == "moe" and sc == 0:
                kb.gdma(taps["t_mt7"], mt[7], waits=[mat_ready[7]])
                kb.gdma(taps["t_mt9"], mt[9], waits=[mat_ready[9]])
            if tap == "moe":
                for ct in range(NCT):
                    kb.gdma(taps["t_ln1"][ct][:, sc * SC:(sc + 1) * SC],
                            ln1[:, ct * SC:(ct + 1) * SC], waits=[prev["ln1"]])
            t_cb = {}
            for s in range(NSL):
                soff = s * SL
                tg = None
                for ic in range(NCT):
                    tg = E("tensor", lambda e, ic=ic, soff=soff: e.matmul(
                        PB[6][0:NE, 0:SL], gwT[:, ic * NE:(ic + 1) * NE],
                        ln1[:, ic * SC + soff: ic * SC + soff + SL],
                        start=(ic == 0), stop=(ic == NCT - 1)),
                        waits=[prev["ln1"], t_gwT] + (ptake(6) if ic == 0 else []),
                        inc="PE" if ic == NCT - 1 else None)
                tsc = E("scalar", lambda e: e.activation(
                    out=sco[0:NE, 0:SL], in_=PB[6][0:NE, 0:SL], func=AF.Sigmoid),
                    waits=[tg] + take("sco"), inc="ACT")
                puse(6, tsc)
                if tap == "moe":
                    ts_t = kb.store(taps["t_sco"][:, sc * SC + soff: sc * SC + soff + SL],
                                    sco[0:NE, 0:SL], waits=[tsc])
                    use("sco", ts_t)
                tse = E("vector", lambda e: e.tensor_scalar_add(
                    sel[0:NE, 0:SL], sco[0:NE, 0:SL], ebias[0:NE, 0:1]),
                    waits=[tsc, t_eb] + take("sel"), inc="DVE")
                tevs = []
                for tb in range(4):
                    tt1 = E("tensor", lambda e, tb=tb: e.transpose(
                        PB[7][0:128, 0:NE], sco[0:NE, tb * 128:(tb + 1) * 128],
                        idn[0:NE, 0:NE]),
                        waits=[tsc, t_idn] + ptake(7), inc="PE")
                    te1 = E("scalar", lambda e, tb=tb: e.copy(
                        out=r_tm[:, tb * NE:(tb + 1) * NE], in_=PB[7][0:128, 0:NE]),
                        waits=[tt1] + (take("r_tm") if tb == 0 else []), inc="ACT")
                    puse(7, te1)
                    tt2 = E("tensor", lambda e, tb=tb: e.transpose(
                        PB[7][0:128, 0:NE], sel[0:NE, tb * 128:(tb + 1) * 128],
                        idn[0:NE, 0:NE]),
                        waits=[tse, t_idn] + ptake(7), inc="PE")
                    te2 = E("scalar", lambda e, tb=tb: e.copy(
                        out=l_tm[:, tb * NE:(tb + 1) * NE], in_=PB[7][0:128, 0:NE]),
                        waits=[tt2] + (take("l_tm") if tb == 0 else []), inc="ACT")
                    puse(7, te2)
                    tevs.append((te1, te2))
                use("sco", tevs[-1][0])
                use("sel", tevs[-1][1])
                tmsk = None
                mw = take("msk")
                rw0 = take("rnk")
                te2 = tevs[3][1]
                tround = None
                for tb in range(4):
                    tround = E("vector", lambda e, tb=tb: e.tensor_single_scalar(
                        rnk[:, tb * NE:(tb + 1) * NE],
                        l_tm[:, tb * NE:(tb + 1) * NE],
                        l_tm[:, tb * NE: tb * NE + 1], ALU.is_lt),
                        waits=[te2, tevs[tb][1]] + rw0,
                        inc="DVE" if tb == 3 else None)
                    rw0 = []
                for jj in range(1, NE):
                    for tb in range(4):
                        tround = E("vector", lambda e, tb=tb, jj=jj: e.scalar_tensor_tensor(
                            rnk[:, tb * NE:(tb + 1) * NE],
                            l_tm[:, tb * NE:(tb + 1) * NE],
                            l_tm[:, tb * NE + jj: tb * NE + jj + 1],
                            rnk[:, tb * NE:(tb + 1) * NE], ALU.is_lt, ALU.add),
                            waits=[tround] if tb == 0 else None,
                            inc="DVE" if tb == 3 else None)
                for jj in range(NE - 1):
                    for tb in range(4):
                        tround = E("vector", lambda e, tb=tb, jj=jj: e.scalar_tensor_tensor(
                            rnk[:, tb * NE + jj + 1:(tb + 1) * NE],
                            l_tm[:, tb * NE + jj + 1:(tb + 1) * NE],
                            l_tm[:, tb * NE + jj: tb * NE + jj + 1],
                            rnk[:, tb * NE + jj + 1:(tb + 1) * NE],
                            ALU.is_equal, ALU.add),
                            waits=[tround] if tb == 0 else None,
                            inc="DVE" if tb == 3 else None)
                for tb in range(4):
                    tmsk = E("vector", lambda e, tb=tb: e.tensor_single_scalar(
                        msk[:, tb * NE:(tb + 1) * NE],
                        rnk[:, tb * NE:(tb + 1) * NE], float(TOPK), ALU.is_lt),
                        waits=mw + [tround], inc="DVE" if tb == 3 else None)
                    mw = []
                use("l_tm", tmsk)
                use("rnk", tmsk)
                if tap == "moe":
                    tm_t = kb.store(
                        taps["t_msk"][:, ((sc * SC + soff) // 128) * NE:
                                      ((sc * SC + soff) // 128 + 4) * NE],
                        msk[:], waits=[tmsk])
                    use("msk", tm_t)
                tcu = None
                for tb in range(4):
                    tcu = E("vector", lambda e, tb=tb: e.tensor_mul(
                        r_tm[:, tb * NE:(tb + 1) * NE],
                        r_tm[:, tb * NE:(tb + 1) * NE],
                        msk[:, tb * NE:(tb + 1) * NE]),
                        waits=[tmsk], inc="DVE" if tb == 3 else None)
                use("msk", tcu)
                tden = None
                dw = take("den")
                for tb in range(4):
                    tden = E("vector", lambda e, tb=tb: e.reduce_sum(
                        den[:, tb:tb + 1], r_tm[:, tb * NE:(tb + 1) * NE], axis=AX.X),
                        waits=[tcu] + dw, inc="DVE" if tb == 3 else None)
                    dw = []
                trcp = E("vector", lambda e: e.reciprocal(den[:, 4:8], den[:, 0:4]),
                         waits=[tden], inc="DVE")
                tcmb = None
                for tb in range(4):
                    tcmb = E("vector", lambda e, tb=tb: e.tensor_scalar(
                        r_tm[:, tb * NE:(tb + 1) * NE],
                        r_tm[:, tb * NE:(tb + 1) * NE],
                        den[:, 4 + tb:5 + tb], ROUTE_SCALE, ALU.mult, ALU.mult),
                        waits=[trcp], inc="DVE" if tb == 3 else None)
                use("den", tcmb)
                tcf = None
                fw = take("cfm")
                for tb in range(4):
                    tt = E("tensor", lambda e, tb=tb: e.transpose(
                        PB[7][0:NE, 0:128], r_tm[:, tb * NE:(tb + 1) * NE], idn[:]),
                        waits=[tcmb, t_idn] + ptake(7), inc="PE")
                    use("r_tm", tt)
                    tcf = E("scalar", lambda e, tb=tb: e.copy(
                        out=cfm[0:NE, tb * 128:(tb + 1) * 128], in_=PB[7][0:NE, 0:128]),
                        waits=[tt] + fw, inc="ACT")
                    fw = []
                    puse(7, tcf)
                if tap == "moe":
                    tc_t = kb.store(taps["t_cmb"][:, sc * SC + soff: sc * SC + soff + SL],
                                    cfm[0:NE, 0:SL], waits=[tcf])
                    use("cfm", tc_t)
                g_idx = (sc * NSL + s) % 2
                tstc = kb.store(cdram[g_idx], cfm[0:NE, 0:SL],
                                waits=[tcf] + take(f"cdram{g_idx}"))
                use("cfm", tstc)
                t_cb[s] = (g_idx, tstc)

            tz = None
            zw = take("y_acc") + take("y_accr")
            for dt in range(NCT):
                tz = E("gpsimd", lambda e, dt=dt: e.memset(
                    y_acc[:, dt * SC:(dt + 1) * SC], 0.0),
                    waits=zw, inc="POOL" if dt == NCT - 1 else None)
                zw = []
            t_yz = tz

            t_hp = {}

            def up_expert(e_id, pp, m_g, m_u, is_shared):
                tcbl = {}
                if not is_shared:
                    for s in range(NSL):
                        g_idx, tstc = t_cb[s]
                        tcbl[s] = kb.emit("sync", lambda e, pp=pp, s=s, g_idx=g_idx, e_id=e_id: e.dma_start(
                            out=cb[pp][s],
                            in_=cdram[g_idx][e_id:e_id + 1, :].broadcast_to([128, SL])),
                            waits=[tstc] + take(f"cb{pp}{s}"), inc="CBS", amt=16)
                        use(f"cdram{g_idx}", tcbl[s])
                thp = None
                for it in range(NCT):
                    bs = it % 2
                    tlg = load_slab(m_g, it, upg[bs][:], f"upg{bs}")
                    tlu = load_slab(m_u, it, upu[bs][:], f"upu{bs}")
                    gb = [0 + 2 * bs, 1 + 2 * bs]
                    ub = [4 + 2 * bs, 5 + 2 * bs]
                    tgm = {}
                    tum = {}
                    for ic in range(NCT):
                        for s in range(NSL):
                            tgm[s] = E("tensor", lambda e2, ic=ic, s=s, bs=bs, gi=gb[s]: e2.matmul(
                                PB[gi][:, 0:SL], upg[bs][:, ic * 128:(ic + 1) * 128],
                                ln1[:, ic * SC + s * SL: ic * SC + (s + 1) * SL],
                                start=(ic == 0), stop=(ic == NCT - 1)),
                                waits=[tlg, prev["ln1"]] + (ptake(gb[s]) if ic == 0 else []),
                                inc="PE" if ic == NCT - 1 else None)
                        for s in range(NSL):
                            tum[s] = E("tensor", lambda e2, ic=ic, s=s, bs=bs, ui=ub[s]: e2.matmul(
                                PB[ui][:, 0:SL], upu[bs][:, ic * 128:(ic + 1) * 128],
                                ln1[:, ic * SC + s * SL: ic * SC + (s + 1) * SL],
                                start=(ic == 0), stop=(ic == NCT - 1)),
                                waits=[tlu] + (ptake(ub[s]) if ic == 0 else []),
                                inc="PE" if ic == NCT - 1 else None)
                    use(f"upg{bs}", tgm[NSL - 1])
                    use(f"upu{bs}", tum[NSL - 1])
                    for s in range(NSL):
                        tsg = E("scalar", lambda e2, s=s, gi=gb[s]: e2.activation(
                            out=sgb[s][:], in_=PB[gi][:, 0:SL], func=AF.Silu),
                            waits=[tgm[s]] + take(f"sgb{s}"), inc="ACT")
                        puse(gb[s], tsg)
                        if is_shared:
                            thu = E("vector", lambda e2, s=s, ui=ub[s]: e2.tensor_copy(
                                out=hub[s][:], in_=PB[ui][:, 0:SL]),
                                waits=[tum[s]] + take(f"hub{s}"), inc="DVE")
                        else:
                            thu = E("vector", lambda e2, s=s, ui=ub[s], pp=pp: e2.tensor_mul(
                                hub[s][:], PB[ui][:, 0:SL], cb[pp][s]),
                                waits=[tum[s], tcbl[s]] + take(f"hub{s}"), inc="DVE")
                            use(f"cb{pp}{s}", thu)
                        puse(ub[s], thu)
                        thp = E("gpsimd", lambda e2, s=s, it=it, pp=pp: e2.tensor_mul(
                            hp[pp][:, hp_off(it, s): hp_off(it, s) + SL],
                            sgb[s][:], hub[s][:]),
                            waits=[tsg, thu] + (take(f"hp{pp}") if (it == 0 and s == 0) else []),
                            inc="POOL")
                        use(f"sgb{s}", thp)
                        use(f"hub{s}", thp)
                        t_hp[(pp, it, s)] = thp
                return thp

            def down_group(members, m_ds, tap_this=False):
                last_pe = None
                for dt in range(NCT):
                    ds = dt % 2
                    db = [0 + 2 * ds, 1 + 2 * ds]
                    tmm = {}
                    first = True
                    for pi_, (pp, m_d) in enumerate(zip(members, m_ds)):
                        tld = load_slab(m_d, dt, wdb[ds][:], f"wdb{ds}")
                        is_last_m = (pi_ == len(members) - 1)
                        for ic in range(NCT):
                            for s in range(NSL):
                                w = [tld, t_hp[(pp, ic, s)]]
                                if first and ic == 0:
                                    w += ptake(db[s])
                                tmm[s] = E("tensor", lambda e2, ic=ic, s=s, ds=ds, pp=pp, di=db[s], st=(first and ic == 0), sp=(is_last_m and ic == NCT - 1): e2.matmul(
                                    PB[di][:, 0:SL], wdb[ds][:, ic * 128:(ic + 1) * 128],
                                    hp[pp][:, hp_off(ic, s): hp_off(ic, s) + SL],
                                    start=st, stop=sp),
                                    waits=w,
                                    inc="PE" if ic == NCT - 1 else None)
                        first = False
                        use(f"wdb{ds}", tmm[NSL - 1])
                        last_pe = tmm[NSL - 1]
                        if False and tap_this and dt == 0 and pi_ == 0:
                            tw = kb.gdma(taps["t_wdb"], wdb[ds][:], waits=[tld])
                            use(f"wdb{ds}", tw)
                    if False and tap_this and dt == 0:
                        tdp = E("scalar", lambda e2, di=db[0]: e2.copy(
                            out=taps and fin[:, 0:SL], in_=PB[di][:, 0:SL]),
                            waits=[tmm[0]] + take("fin"), inc="ACT")
                        puse(db[0], tdp)
                        tps = kb.store(taps["t_dps"], fin[:, 0:SL], waits=[tdp])
                        use("fin", tps)
                    for s in range(NSL):
                        ty = E("vector", lambda e2, dt=dt, s=s, di=db[s]: e2.tensor_add(
                            y_acc[:, dt * SC + s * SL: dt * SC + (s + 1) * SL],
                            y_acc[:, dt * SC + s * SL: dt * SC + (s + 1) * SL],
                            PB[di][:, 0:SL]),
                            waits=[tmm[s], t_yz], inc="DVE")
                        puse(db[s], ty)
                        prev["y_done"] = ty
                for pp in members:
                    use(f"hp{pp}", last_pe)

            pairs = range(NE // 2) if MOE_PAIRS is None else MOE_PAIRS
            for p in pairs:
                e0, e1 = 2 * p, 2 * p + 1
                th0 = up_expert(e0, 0, 7 + 3 * e0, 7 + 3 * e0 + 1, False)
                th1 = up_expert(e1, 1, 7 + 3 * e1, 7 + 3 * e1 + 1, False)
                tap_p = (tap == "moe" and sc == 0 and p == (list(pairs)[0] if MOE_PAIRS else 0))
                if tap_p:
                    for pp, th in ((0, th0), (1, th1)):
                        tk = kb.gdma(taps["t_hp"][pp], hp[pp], waits=[th])
                        use(f"hp{pp}", tk)
                down_group([0, 1], [7 + 3 * e0 + 2, 7 + 3 * e1 + 2], tap_this=tap_p)

            if MOE_SHARED:
                if sc == 0 and 4 not in mat_ready:
                    for m_s in (4, 5, 6):
                        phase0(m_s)
                up_expert(None, 0, 4, 5, True)
                down_group([0], [6])

            if tap == "moe":
                for ct in range(NCT):
                    ty_t = kb.store(taps["t_ymoe"][ct][:, sc * SC:(sc + 1) * SC],
                                    y_acc[:, ct * SC:(ct + 1) * SC],
                                    waits=[prev["y_done"]])
                    use("y_accr", ty_t)
            for s in range(NSL):
                soff = s * SL
                tgl = sc * SC + soff
                tx1l = {}
                for ct in range(NCT):
                    tx1l[ct] = kb.load(
                        x1r[:, ct * SL:(ct + 1) * SL],
                        x1_dram[ct][:, tgl:tgl + SL],
                        waits=take(f"x1r{ct}") + [prev.get("x1st")])
                tcy = None
                cw = take("fin")
                for ct in range(NCT):
                    tcy = E("scalar", lambda e2, ct=ct, soff=soff: e2.copy(
                        out=ybf[:, ct * SL:(ct + 1) * SL],
                        in_=y_acc[:, ct * SC + soff: ct * SC + soff + SL]),
                        waits=[prev["y_done"]] + cw, inc="ACT" if ct == NCT - 1 else None)
                    cw = []
                tsy = None
                for ct in range(NCT):
                    tsy = E("scalar", lambda e2, ct=ct: e2.activation(
                        out=ysq[:, ct * SL:(ct + 1) * SL],
                        in_=ybf[:, ct * SL:(ct + 1) * SL], func=AF.Square),
                        inc="ACT" if ct == NCT - 1 else None)
                tmmA = None
                for ct in range(NCT):
                    tmmA = E("tensor", lambda e2, ct=ct: e2.matmul(
                        PB[6][:, 0:SL], ones_bf[:], ybf[:, ct * SL:(ct + 1) * SL],
                        start=(ct == 0), stop=(ct == NCT - 1)),
                        waits=[tcy] + (ptake(6) if ct == 0 else []),
                        inc="PE" if ct == NCT - 1 else None)
                tmmB = None
                for ct in range(NCT):
                    tmmB = E("tensor", lambda e2, ct=ct: e2.matmul(
                        PB[7][:, 0:SL], ones_bf[:], ysq[:, ct * SL:(ct + 1) * SL],
                        start=(ct == 0), stop=(ct == NCT - 1)),
                        waits=[tsy] + (ptake(7) if ct == 0 else []),
                        inc="PE" if ct == NCT - 1 else None)
                mu3 = stats[:, s_mu * SUB: s_mu * SUB + SL]
                e23 = stats[:, s_var * SUB: s_var * SUB + SL]
                rstd3 = stats[:, s_rstd * SUB: s_rstd * SUB + SL]
                sp3 = stats[:, s_vi * SUB: s_vi * SUB + SL]
                sp4 = stats[:, s_mu2 * SUB: s_mu2 * SUB + SL]
                t_mu3 = E("scalar", lambda e2: e2.mul(mu3, PB[6][:, 0:SL], np.float32(1.0 / C)),
                          waits=[tmmA], inc="ACT")
                puse(6, t_mu3)
                t_e23 = E("scalar", lambda e2: e2.mul(e23, PB[7][:, 0:SL], np.float32(1.0 / C)),
                          waits=[tmmB], inc="ACT")
                puse(7, t_e23)
                t_m3 = E("gpsimd", lambda e2: e2.tensor_mul(sp3, mu3, mu3),
                         waits=[t_mu3], inc="POOL")
                t_v3 = E("vector", lambda e2: e2.tensor_sub(e23, e23, sp3),
                         waits=[t_e23, t_m3], inc="DVE")
                t_sd3 = E("scalar", lambda e2: e2.activation(
                    out=sp4, in_=e23, func=AF.Sqrt, bias=np.float32(LN_EPS)),
                    waits=[t_v3], inc="ACT")
                t_r3 = E("vector", lambda e2: e2.reciprocal(rstd3, sp4),
                         waits=[t_sd3], inc="DVE")
                finf = fin
                tx2 = {}
                fw = [tmmA, tmmB]
                tf1 = None
                for ct in range(NCT):
                    tf1 = E("gpsimd", lambda e2, ct=ct, soff=soff: e2.tensor_sub(
                        finf[:, ct * SL:(ct + 1) * SL],
                        y_acc[:, ct * SC + soff: ct * SC + soff + SL], mu3),
                        waits=[t_mu3] + fw,
                        inc="POOL" if ct == NCT - 1 else None)
                    fw = []
                tf2 = None
                for ct in range(NCT):
                    tf2 = E("gpsimd", lambda e2, ct=ct: e2.tensor_mul(
                        finf[:, ct * SL:(ct + 1) * SL],
                        finf[:, ct * SL:(ct + 1) * SL], rstd3),
                        waits=[t_r3, tf1],
                        inc="POOL" if ct == NCT - 1 else None)
                for ct in range(NCT):
                    tx2[ct] = E("gpsimd", lambda e2, ct=ct: e2.tensor_add(
                        x1r[:, ct * SL:(ct + 1) * SL],
                        x1r[:, ct * SL:(ct + 1) * SL],
                        finf[:, ct * SL:(ct + 1) * SL]),
                        waits=[tx1l[ct], tf2], inc="POOL")
                    use("fin", tx2[ct])
                    use("y_accr", tx2[ct])
                for tb in range(4):
                    bs = tb % 2
                    tevl = None
                    for ct in range(NCT):
                        pi = 6 + (ct % 2)
                        ttr = E("tensor", lambda e2, tb=tb, ct=ct, pi=pi: e2.transpose(
                            PB[pi][:, 0:128],
                            x1r[:, ct * SL + tb * 128: ct * SL + (tb + 1) * 128],
                            idn[:]),
                            waits=[tx2[ct], t_idn] + ptake(pi), inc="PE")
                        use(f"x1r{ct}", ttr)
                        tevl = E("scalar", lambda e2, ct=ct, pi=pi, bs=bs: e2.copy(
                            out=x2tm[bs][:, ct * 128:(ct + 1) * 128],
                            in_=PB[pi][:, 0:128]),
                            waits=[ttr] + (take(f"x2tm{bs}") if ct == 0 else []),
                            inc="ACT")
                        puse(pi, tevl)
                    tst = kb.store(y_d[tgl + tb * 128: tgl + (tb + 1) * 128, :],
                                   x2tm[bs][:], waits=[tevl])
                    use(f"x2tm{bs}", tst)
            use("ln1buf", prev["y_done"])
            use("y_accr", prev["y_done"])

        p0_queue = [7 + 3 * e + j for e in range(2, NE) for j in range(3)] + [4, 5, 6]
        for sc in range(NSC):
            kb.set_epoch(sc)
            for jj in range(SUBS):
                attn_sub(sc, sc * SUBS + jj)
            nchunk = (len(p0_queue) + NSC - 1 - sc) // (NSC - sc) if sc < NSC else 0
            for _ in range(min(nchunk, len(p0_queue))):
                phase0(p0_queue.pop(0))
        if stage >= 2:
            kb.barrier()
            for sc in range(NSC):
                kb.set_epoch(sc)
                moe_sc(sc)
        kb.barrier()

        allst = [(s, kb.cnt[s]) for s in kb.sem
                 if (s.startswith("ST") or s == "GST") and kb.cnt[s] > 0]
        if stage == 1:
            tlast = kb.gdma(_flat3(y_d), x1_dram, waits=allst)
            kb.emit("gpsimd", lambda e: e.nop(), waits=[tlast])
        else:
            kb.emit("scalar", lambda e: e.nop(), waits=allst)

        kb.finalize(block)
    return nc


def _flat3(y_ap):
    return y_ap.rearrange("a c -> (a c)").rearrange(
        "(ct p t) -> ct p t", ct=NCT, p=128, t=T)


def kernel(**inputs):
    nc = build_program(stage=2)
    xs = np.ascontiguousarray(inputs["L0_attn_x"])
    names = {
        "time_mix_k": "L0_attn_time_mix_k", "time_mix_v": "L0_attn_time_mix_v",
        "time_mix_r": "L0_attn_time_mix_r", "time_decay": "L0_attn_time_decay",
        "time_first": "L0_attn_time_first", "Wk": "L0_attn_Wk", "Wv": "L0_attn_Wv",
        "Wr": "L0_attn_Wr", "Wo": "L0_attn_Wo", "ln_w": "L0_attn_ln_w",
        "ln_b": "L0_attn_ln_b", "gate_w": "L0_ffn_gate_w",
        "expert_bias": "L0_ffn_expert_bias", "w_gate": "L0_ffn_w_gate",
        "w_up": "L0_ffn_w_up", "w_down": "L0_ffn_w_down",
        "shared_gate_w": "L0_ffn_shared_gate_w", "shared_up_w": "L0_ffn_shared_up_w",
        "shared_down_w": "L0_ffn_shared_down_w",
    }
    shared = {k: np.ascontiguousarray(np.asarray(inputs[v], dtype=np.float32))
              for k, v in names.items()}
    in_maps = [dict(shared, x=np.ascontiguousarray(xs[b])) for b in range(8)]
    res = run_bass_kernel_spmd(nc, in_maps, core_ids=list(range(8)))
    return np.stack([np.asarray(r["y"]) for r in res.results], axis=0)


# version 11
# speedup vs baseline: 1.0495x; 1.0495x over previous
import numpy as np
from contextlib import ExitStack

import concourse.bass as bass
import concourse.mybir as mybir
from concourse.bass_utils import run_bass_kernel_spmd

F32 = mybir.dt.float32
BF16 = mybir.dt.bfloat16
AF = mybir.ActivationFunctionType
ALU = mybir.AluOpType
AX = mybir.AxisListType

T = 8192
C = 1024
NCT = 8
SUB = 256
SC = 1024
NSC = T // SC
SUBS = SC // SUB
SL = 512
NSL = SC // SL
NE = 16
TOPK = 8
ROUTE_SCALE = 2.826
LN_EPS = 1e-5
NMAT = 55
MOE_PAIRS = None
MOE_SHARED = True
RACE_CHECK = True


class KB:
    ENGS = ("tensor", "vector", "scalar", "gpsimd", "sync")

    def __init__(self, nc, ctx):
        self.nc = nc
        self.steps = []
        self.sem = {}
        self.cnt = {}
        self.lastw = {e: {} for e in self.ENGS}
        names = ["GPE", "GDVE", "GACT", "GPOOL", "GLD", "GST", "GGP", "P0S", "CBS", "X1S", "TRS", "L1S"]
        for sc in range(NSC):
            for s in ("PE", "DVE", "ACT", "POOL", "LD", "ST", "GP"):
                names.append(f"{s}{sc}")
        for n in names:
            self.sem[n] = ctx.enter_context(nc.semaphore(n))
            self.cnt[n] = 0
        self.cur = "G"

    def S(self, base):
        return (("G" + base) if self.cur == "G" else (base + self.cur))

    def set_epoch(self, sc):
        self.cur = "G" if sc is None else str(sc)

    def emit(self, engine, fn, waits=None, inc=None, amt=1):
        w = {}
        for t in (waits or ()):
            if t is None:
                continue
            s, v = t
            if v > 0 and self.lastw[engine].get(s, 0) < v:
                w[s] = max(w.get(s, 0), v)
        for s, v in w.items():
            self.lastw[engine][s] = v
        if inc is not None:
            s = inc if inc in self.sem else self.S(inc)
            self.cnt[s] += amt
            self.steps.append((engine, fn, w, (s, amt)))
            return (s, self.cnt[s])
        self.steps.append((engine, fn, w, None))
        return None

    def load(self, dst, src, waits=None):
        return self.emit("sync", lambda e: e.dma_start(out=dst, in_=src),
                         waits, inc="LD", amt=16)

    def loadT(self, dst, src, waits=None):
        return self.emit("sync", lambda e: e.dma_start(out=dst, in_=src, transpose=True),
                         waits, inc="LD", amt=16)

    def store(self, dst, src, waits=None):
        return self.emit("scalar", lambda e: e.dma_start(out=dst, in_=src),
                         waits, inc="ST", amt=16)

    def gdma(self, dst, src, waits=None):
        return self.emit("gpsimd", lambda e: e.dma_start(out=dst, in_=src),
                         waits, inc="GP", amt=16)

    def barrier(self):
        cts = [(s, self.cnt[s]) for s in self.sem if self.cnt[s] > 0]
        for e in self.ENGS:
            self.emit(e, lambda eng: eng.nop(), cts)

    def finalize(self, block):
        per = {e: [] for e in self.ENGS}
        for st in self.steps:
            per[st[0]].append(st)

        def mk(lst):
            def run(eng):
                for (_, fn, w, inc) in lst:
                    for s, v in w.items():
                        eng.wait_ge(self.sem[s], v)
                    ins = fn(eng)
                    if inc is not None:
                        ins.then_inc(self.sem[inc[0]], inc[1])
            return run
        block.tensor(mk(per["tensor"]))
        block.vector(mk(per["vector"]))
        block.scalar(mk(per["scalar"]))
        block.gpsimd(mk(per["gpsimd"]))
        block.sync(mk(per["sync"]))


class Arena:
    def __init__(self, nc, ctx, name, cols):
        self.t = ctx.enter_context(
            nc.sbuf_tensor(name, [128, cols], F32, is_transpose_dest=True))
        self.cols = cols
        self.base = 0

    def f32(self, n):
        assert self.base + n <= self.cols, f"arena overflow {self.base}+{n}>{self.cols}"
        v = self.t[:, self.base:self.base + n]
        self.base += n
        return v

    def bf16(self, n):
        assert n % 2 == 0
        v = self.f32(n // 2).bitcast(BF16)
        assert v.shape == (128, n), f"bitcast shape {v.shape}"
        return v

    def mark(self):
        return self.base

    def reset(self, m):
        self.base = m


def build_program(stage=2, tap=None):
    nc = bass.Bass("TRN2", target_bir_lowering=False, debug=False,
                   detect_race_conditions=RACE_CHECK)

    x_d = nc.dram_tensor("x", [T, C], F32, kind="ExternalInput").ap()
    y_d = nc.dram_tensor("y", [T, C], F32, kind="ExternalOutput").ap()
    p_mk = nc.dram_tensor("time_mix_k", [C], F32, kind="ExternalInput").ap()
    p_mv = nc.dram_tensor("time_mix_v", [C], F32, kind="ExternalInput").ap()
    p_mr = nc.dram_tensor("time_mix_r", [C], F32, kind="ExternalInput").ap()
    p_td = nc.dram_tensor("time_decay", [C], F32, kind="ExternalInput").ap()
    p_tf = nc.dram_tensor("time_first", [C], F32, kind="ExternalInput").ap()
    p_lw = nc.dram_tensor("ln_w", [C], F32, kind="ExternalInput").ap()
    p_lb = nc.dram_tensor("ln_b", [C], F32, kind="ExternalInput").ap()
    wk_d = nc.dram_tensor("Wk", [C, C], F32, kind="ExternalInput").ap()
    wv_d = nc.dram_tensor("Wv", [C, C], F32, kind="ExternalInput").ap()
    wr_d = nc.dram_tensor("Wr", [C, C], F32, kind="ExternalInput").ap()
    wo_d = nc.dram_tensor("Wo", [C, C], F32, kind="ExternalInput").ap()
    gw_d = nc.dram_tensor("gate_w", [NE, C], F32, kind="ExternalInput").ap()
    eb_d = nc.dram_tensor("expert_bias", [NE], F32, kind="ExternalInput").ap()
    wg_d = nc.dram_tensor("w_gate", [NE, C, C], F32, kind="ExternalInput").ap()
    wu_d = nc.dram_tensor("w_up", [NE, C, C], F32, kind="ExternalInput").ap()
    wd_d = nc.dram_tensor("w_down", [NE, C, C], F32, kind="ExternalInput").ap()
    sg_d = nc.dram_tensor("shared_gate_w", [C, C], F32, kind="ExternalInput").ap()
    su_d = nc.dram_tensor("shared_up_w", [C, C], F32, kind="ExternalInput").ap()
    sd_d = nc.dram_tensor("shared_down_w", [C, C], F32, kind="ExternalInput").ap()

    taps = {}
    if tap == "moe":
        taps["t_ln1"] = nc.dram_tensor("t_ln1", [NCT, 128, T], F32, kind="ExternalOutput").ap()
        taps["t_ymoe"] = nc.dram_tensor("t_ymoe", [NCT, 128, T], F32, kind="ExternalOutput").ap()
        taps["t_sco"] = nc.dram_tensor("t_sco", [NE, T], F32, kind="ExternalOutput").ap()
        taps["t_cmb"] = nc.dram_tensor("t_cmb", [NE, T], F32, kind="ExternalOutput").ap()
        taps["t_msk"] = nc.dram_tensor("t_msk", [128, (T // 128) * NE], F32, kind="ExternalOutput").ap()
        taps["t_hp"] = nc.dram_tensor("t_hp", [2, 128, NSL * NCT * SL], F32, kind="ExternalOutput").ap()
        taps["t_mt7"] = nc.dram_tensor("t_mt7", [8, 128, 8, 128], F32, kind="ExternalOutput").ap()
        taps["t_wdb"] = nc.dram_tensor("t_wdb", [128, 1024], F32, kind="ExternalOutput").ap()
        taps["t_dps"] = nc.dram_tensor("t_dps", [128, SL], F32, kind="ExternalOutput").ap()
        taps["t_mt9"] = nc.dram_tensor("t_mt9", [8, 128, 8, 128], F32, kind="ExternalOutput").ap()
    bfc = nc.dram_tensor("bfc", [NMAT, C, C], BF16, kind="Internal").ap()
    mt = nc.dram_tensor("mt", [NMAT, 8, 128, 8, 128], BF16, kind="Internal").ap()
    gw_bf = nc.dram_tensor("gw_bf", [NE, C], BF16, kind="Internal").ap()
    x1_dram = nc.dram_tensor("x1d", [NCT, 128, T], F32, kind="Internal").ap()
    ln1_dram = nc.dram_tensor("l1d", [NCT, 128, T], BF16, kind="Internal").ap()
    cdram = nc.dram_tensor("cdram", [2, NE, SL], F32, kind="Internal").ap()

    def mat_src(m):
        if m < 4:
            return (wk_d, wv_d, wr_d, wo_d)[m]
        if m < 7:
            return (sg_d, su_d, sd_d)[m - 4]
        e, j = divmod(m - 7, 3)
        return (wg_d, wu_d, wd_d)[j][e]

    ctx = ExitStack()
    with ctx:
        sbt = lambda n, s, d, **kw: ctx.enter_context(nc.sbuf_tensor(n, s, d, **kw))
        ln1 = sbt("ln1", [128, NCT * SC], BF16)
        decay_b = sbt("decay_b", [128, NCT * SUB], F32)
        ones_bf = sbt("ones_bf", [128, 128], BF16)
        idn = sbt("idn", [128, 128], F32)
        prm = sbt("prm", [128, 64], F32)
        ebias = sbt("ebias", [128, 1], F32)
        carryS = sbt("carryS", [128, NCT], F32)
        xtc = sbt("xtc", [128, NCT], F32)
        ptmp = sbt("ptmp", [128, 128], F32)
        gwT = sbt("gwT", [128, NCT * NE], BF16, is_transpose_dest=True)
        tbuf = [sbt(f"tbuf{i}", [128, 1024], BF16, is_transpose_dest=True) for i in range(2)]
        stats = sbt("stats", [128, 12 * SUB], F32)
        x2tm = [sbt(f"x2tm{i}", [128, 1024], F32) for i in range(2)]
        cfm = sbt("cfm", [128, SL], F32)
        sco = sbt("sco", [128, SL], F32)
        sel = sbt("sel", [128, SL], F32)

        STT = lambda i: stats[:, i * SUB:(i + 1) * SUB]
        (s_mu, s_e2, s_var, s_sd, s_rstd, s_msq, s_vi, s_si,
         s_mu2, s_rstd2, s_sp1, s_sp2) = range(12)

        AR = Arena(nc, ctx, "arena", 38912)
        shared0 = AR.mark()
        attnW = [AR.bf16(8192) for _ in range(4)]
        xtm = AR.f32(2048)
        xfm = AR.f32(2048)
        xbf = AR.bf16(2048)
        ln0 = AR.f32(2048)
        xt = AR.f32(NCT * (SUB + 1))
        kxb = AR.bf16(2048)
        vxb = AR.bf16(2048)
        rxb = AR.bf16(2048)
        Kf = AR.f32(2048)
        Vb = AR.bf16(2048)
        Rb = AR.bf16(2048)
        Sf = AR.f32(2048)
        xsq = rxb
        rkb = vxb
        attn_end = AR.mark()
        AR.reset(shared0)
        y_acc = AR.f32(NCT * SC)
        upg = [AR.bf16(1024) for _ in range(2)]
        upu = [AR.bf16(1024) for _ in range(2)]
        wdb = [AR.bf16(1024) for _ in range(2)]
        hp = [AR.bf16(NSL * NCT * SL) for _ in range(2)]
        sgb = [AR.f32(SL) for _ in range(2)]
        hub = [AR.f32(SL) for _ in range(2)]
        cb = [[AR.f32(SL) for _ in range(2)] for _ in range(2)]
        r_tm = AR.f32(64)
        l_tm = AR.f32(64)
        rnk = AR.f32(64)
        msk = AR.f32(64)
        den = AR.f32(8)
        x1r = AR.f32(NCT * SL)
        fin = AR.f32(NCT * SL)
        ybf = fin[:, 0:NCT * SL // 2].bitcast(BF16)
        ysq = fin[:, NCT * SL // 2:].bitcast(BF16)
        moe_end = AR.mark()
        assert max(attn_end, moe_end) <= AR.cols, (attn_end, moe_end)

        def hp_off(it, s):
            return (s * NCT + it) * SL

        PB = [ctx.enter_context(nc.psum_tensor(f"pb{i}", [128, 512], F32))
              for i in range(8)]

        kb = KB(nc, ctx)
        block = ctx.enter_context(nc.Block())
        E = kb.emit

        war = {}

        def use(name, tick):
            if tick is not None:
                war.setdefault(name, []).append(tick)

        def take(name):
            return war.pop(name, [])

        def puse(i, tick):
            use(f"PS{i}", tick)

        def ptake(i):
            return take(f"PS{i}")

        ct_s = lambda buf, ct: buf[:, ct * SUB:(ct + 1) * SUB]

        kb.set_epoch(None)
        t_ms0 = E("gpsimd", lambda e: e.memset(idn[:], 0.0), inc="POOL")
        E("gpsimd", lambda e: e.memset(ones_bf[:], 1.0))
        E("gpsimd", lambda e: e.memset(carryS[:], 0.0))
        t_idn = E("gpsimd", lambda e: e.affine_select(
            out=idn[:], in_=idn[:], compare_op=ALU.not_equal, fill=1.0,
            base=0, pattern=[[-1, 128]], channel_multiplier=1),
            waits=[t_ms0], inc="POOL")
        t_zero = E("gpsimd", lambda e: e.memset(xtc[:], 0.0), inc="POOL")
        for _ct in range(NCT):
            t_zero = E("gpsimd", lambda e, ct=_ct: e.memset(
                xt[:, ct * 257: ct * 257 + 1], 0.0), inc="POOL")
        t_ones = t_idn

        pv = [(p_mk, 0), (p_mv, 8), (p_mr, 16), (p_td, 56), (p_tf, 56),
              (p_lw, 40), (p_lb, 48)]
        tt = None
        t_actp = None
        for i, (pd, dst) in enumerate(pv):
            tl = kb.load(ptmp[0:8, :], pd.rearrange("(a b) -> a b", b=128),
                         waits=[tt] if tt else None)
            tt = E("tensor", lambda e: e.transpose(
                PB[7][0:128, 0:8], ptmp[0:8, :], idn[0:8, 0:8]),
                waits=[tl, t_idn] + ptake(7), inc="PE")
            tc = E("scalar", lambda e, dst=dst: e.copy(
                out=prm[:, dst:dst + 8], in_=PB[7][0:128, 0:8]),
                waits=[tt, t_actp], inc="ACT")
            puse(7, tc)
            if i == 3:
                t_actp = E("scalar", lambda e: e.activation(
                    out=prm[:, 24:32], in_=prm[:, 56:64], func=AF.Sigmoid),
                    waits=[tc], inc="ACT")
            elif i == 4:
                t_actp = E("scalar", lambda e: e.activation(
                    out=prm[:, 32:40], in_=prm[:, 56:64], func=AF.Exp),
                    waits=[tc], inc="ACT")
        t_prm = ("GACT", kb.cnt["GACT"])
        t_one = E("gpsimd", lambda e: e.memset(STT(s_sp1), 1.0), inc="POOL")
        t_decb = None
        for ct in range(NCT):
            t_decb = E("vector", lambda e, ct=ct: e.tensor_scalar_mul(
                decay_b[:, ct * SUB:(ct + 1) * SUB], STT(s_sp1),
                prm[:, 24 + ct:25 + ct]), waits=[t_prm, t_one],
                inc="DVE" if ct == NCT - 1 else None)
        t_eb = kb.load(ebias[0:NE, 0:1], eb_d.rearrange("(a b) -> a b", b=1))

        kb.set_epoch(0)
        t_cast = {}
        t_gw_cast = kb.gdma(gw_bf, gw_d)
        _prev_cast = [t_gw_cast]

        def emit_cast(m):
            t_cast[m] = kb.gdma(bfc[m], mat_src(m), waits=[_prev_cast[0]])
            _prev_cast[0] = t_cast[m]

        mat_ready = {}

        def phase0(m, to_sbuf=None):
            if m not in t_cast:
                emit_cast(m)
            if to_sbuf is not None:
                dest = to_sbuf
                cp_ticks = []
                for k in range(4):
                    t0_ = kb.emit("sync", lambda e, m=m, k=k: e.dma_start(
                        out=tbuf[0][:], in_=bfc[m][:, (2 * k) * 128:(2 * k + 1) * 128],
                        transpose=True),
                        waits=[t_cast[m]] + take("tbuf0"), inc="TRS", amt=16)
                    t1_ = kb.emit("sync", lambda e, m=m, k=k: e.dma_start(
                        out=tbuf[1][:], in_=bfc[m][:, (2 * k + 1) * 128:(2 * k + 2) * 128],
                        transpose=True),
                        waits=take("tbuf1"), inc="TRS", amt=16)
                    last = None
                    for b in range(2):
                        ic = 2 * k + b
                        for ot in range(8):
                            last = E("gpsimd", lambda e, b=b, ot=ot, ic=ic, dest=dest: e.tensor_copy(
                                out=dest[:, (ot * 8 + ic) * 128:(ot * 8 + ic + 1) * 128],
                                in_=tbuf[b][:, ot * 128:(ot + 1) * 128]),
                                waits=[t1_],
                                inc="POOL" if ot == 7 else None)
                        use(f"tbuf{b}", last)
                    cp_ticks.append(last)
                st = None
                for ot in range(8):
                    st = kb.emit("scalar", lambda e, ot=ot, dest=dest: e.dma_start(
                        out=mt[m][ot], in_=dest[:, ot * 1024:(ot + 1) * 1024]),
                        waits=cp_ticks, inc="P0S", amt=16)
                mat_ready[m] = st
                return cp_ticks[-1]
            st = None
            for k in range(4):
                t0_ = kb.emit("sync", lambda e, m=m, k=k: e.dma_start(
                    out=tbuf[0][:], in_=bfc[m][:, (2 * k) * 128:(2 * k + 1) * 128],
                    transpose=True),
                    waits=[t_cast[m]] + take("tbuf0"), inc="TRS", amt=16)
                t1_ = kb.emit("sync", lambda e, m=m, k=k: e.dma_start(
                    out=tbuf[1][:], in_=bfc[m][:, (2 * k + 1) * 128:(2 * k + 2) * 128],
                    transpose=True),
                    waits=take("tbuf1"), inc="TRS", amt=16)
                for b in range(2):
                    ic = 2 * k + b
                    for ot in range(8):
                        st = kb.emit("scalar", lambda e, m=m, ot=ot, ic=ic, b=b: e.dma_start(
                            out=mt[m][ot][:, ic, :],
                            in_=tbuf[b][:, ot * 128:(ot + 1) * 128]),
                            waits=[t1_], inc="P0S", amt=16)
                use("tbuf0", st)
                use("tbuf1", st)
            mat_ready[m] = st
            return st

        attnW_p0 = [phase0(m, to_sbuf=attnW[m]) for m in range(4)]
        attnw_tick = [attnW_p0[m] for m in range(4)]
        t_gwT = None
        for ic in range(8):
            t_gwT = kb.loadT(gwT[:, ic * NE:(ic + 1) * NE],
                             gw_bf[:, ic * 128:(ic + 1) * 128],
                             waits=[t_gw_cast])
        for ee in range(2):
            for jj in range(3):
                phase0(7 + 3 * ee + jj)

        prev = {"ln1": None, "y_done": None}

        def ln_stats(src_bf, src_sq, mu_slot, rstd_slot, t_cast_done, t_sq_done):
            tmmA = None
            for ct in range(NCT):
                tmmA = E("tensor", lambda e, ct=ct: e.matmul(
                    PB[0][:, 0:SUB], ones_bf[:], ct_s(src_bf, ct),
                    start=(ct == 0), stop=(ct == NCT - 1)),
                    waits=[t_cast_done, t_ones] + (ptake(0) if ct == 0 else []),
                    inc="PE" if ct == NCT - 1 else None)
            tmmB = None
            for ct in range(NCT):
                tmmB = E("tensor", lambda e, ct=ct: e.matmul(
                    PB[1][:, 0:SUB], ones_bf[:], ct_s(src_sq, ct),
                    start=(ct == 0), stop=(ct == NCT - 1)),
                    waits=[t_sq_done] + (ptake(1) if ct == 0 else []),
                    inc="PE" if ct == NCT - 1 else None)
            t_mu = E("scalar", lambda e, mu_slot=mu_slot: e.mul(
                STT(mu_slot), PB[0][:, 0:SUB], np.float32(1.0 / C)), waits=[tmmA], inc="ACT")
            puse(0, t_mu)
            t_e2 = E("scalar", lambda e: e.mul(
                STT(s_e2), PB[1][:, 0:SUB], np.float32(1.0 / C)), waits=[tmmB], inc="ACT")
            puse(1, t_e2)
            t_msq = E("gpsimd", lambda e, mu_slot=mu_slot: e.tensor_mul(
                STT(s_msq), STT(mu_slot), STT(mu_slot)), waits=[t_mu], inc="POOL")
            t_var = E("vector", lambda e: e.tensor_sub(
                STT(s_var), STT(s_e2), STT(s_msq)), waits=[t_e2, t_msq], inc="DVE")
            t_sd = E("scalar", lambda e: e.activation(
                out=STT(s_sd), in_=STT(s_var), func=AF.Sqrt, bias=np.float32(LN_EPS)),
                waits=[t_var], inc="ACT")
            t_rstd = E("vector", lambda e, rstd_slot=rstd_slot: e.reciprocal(
                STT(rstd_slot), STT(s_sd)), waits=[t_sd], inc="DVE")
            return t_mu, t_var, t_rstd, tmmA, tmmB

        def attn_sub(sc, j):
            t0 = j * SUB
            lts = []
            for tb in range(2):
                lts.append(kb.load(
                    xtm[:, tb * 1024:(tb + 1) * 1024],
                    x_d[t0 + tb * 128: t0 + (tb + 1) * 128, :],
                    waits=take("xtm") if tb == 0 else []))
            t_xfm = None
            xw = take("xfm")
            for tb in range(2):
                for ct in range(NCT):
                    pi = 6 + ((tb * NCT + ct) % 2)
                    tt = E("tensor", lambda e, tb=tb, ct=ct, pi=pi: e.transpose(
                        PB[pi][:, 0:128],
                        xtm[:, tb * 1024 + ct * 128: tb * 1024 + (ct + 1) * 128],
                        idn[:]),
                        waits=[lts[tb], t_idn] + ptake(pi), inc="PE")
                    use("xtm", tt)
                    t_xfm = E("scalar", lambda e, tb=tb, ct=ct, pi=pi: e.copy(
                        out=xfm[:, ct * SUB + tb * 128: ct * SUB + (tb + 1) * 128],
                        in_=PB[pi][:, 0:128]),
                        waits=[tt] + xw, inc="ACT")
                    xw = []
                    puse(pi, t_xfm)
            tcast = None
            cw = take("xbf")
            for ct in range(NCT):
                tcast = E("scalar", lambda e, ct=ct: e.copy(
                    out=ct_s(xbf, ct), in_=ct_s(xfm, ct)),
                    waits=[t_xfm] + cw, inc="ACT" if ct == NCT - 1 else None)
                cw = []
            tsq = None
            sw = take("xsq")
            for ct in range(NCT):
                tsq = E("scalar", lambda e, ct=ct: e.activation(
                    out=ct_s(xsq, ct), in_=ct_s(xbf, ct), func=AF.Square),
                    waits=sw, inc="ACT" if ct == NCT - 1 else None)
                sw = []
            t_mu, t_var, t_rstd, tmmA, tmmB = ln_stats(xbf, xsq, s_mu, s_rstd, tcast, tsq)
            use("xbf", tmmA)
            use("rxq", tmmB)
            tv1 = E("gpsimd", lambda e: e.tensor_mul(STT(s_sp1), STT(s_var), STT(s_rstd)),
                    waits=[t_rstd], inc="POOL")
            tv2 = E("gpsimd", lambda e: e.tensor_mul(STT(s_vi), STT(s_sp1), STT(s_rstd)),
                    waits=[tv1], inc="POOL")
            tsd2 = E("scalar", lambda e: e.activation(
                out=STT(s_sp2), in_=STT(s_vi), func=AF.Sqrt, bias=np.float32(LN_EPS)),
                waits=[tv2], inc="ACT")
            t_si = E("vector", lambda e: e.reciprocal(STT(s_si), STT(s_sp2)),
                     waits=[tsd2], inc="DVE")
            tl0 = None
            lw = take("ln0")
            for ct in range(NCT):
                tl0 = E("vector", lambda e, ct=ct: e.tensor_sub(
                    ct_s(ln0, ct), ct_s(xfm, ct), STT(s_mu)),
                    waits=[t_mu, t_xfm] + lw,
                    inc="DVE" if ct == NCT - 1 else None)
                lw = []
            tsubp = tl0
            for ct in range(NCT):
                tl0 = E("vector", lambda e, ct=ct: e.tensor_mul(
                    ct_s(ln0, ct), ct_s(ln0, ct), STT(s_rstd)),
                    waits=[t_rstd, tsubp], inc="DVE" if ct == NCT - 1 else None)
            use("xfm", tl0)
            txt = None
            tw = take("Sf")
            xtw = take("xt")
            for ct in range(NCT):
                tt1 = E("gpsimd", lambda e, ct=ct: e.tensor_mul(
                    ct_s(Sf, ct), ct_s(ln0, ct), STT(s_si)),
                    waits=[tl0, t_si] + tw, inc="POOL")
                tw = []
                txt = E("vector", lambda e, ct=ct: e.tensor_scalar(
                    xt[:, ct * 257 + 1: ct * 257 + 257], ct_s(Sf, ct),
                    prm[:, 40 + ct:41 + ct], prm[:, 48 + ct:49 + ct],
                    ALU.mult, ALU.add),
                    waits=[tt1, t_prm] + xtw,
                    inc="DVE" if ct == NCT - 1 else None)
                xtw = []
            use("Sf", txt)
            tdm = None
            kw = take("Kf")
            for ct in range(NCT):
                tdm = E("gpsimd", lambda e, ct=ct: e.tensor_sub(
                    ct_s(Kf, ct), xt[:, ct * 257 + 1: ct * 257 + 257],
                    xt[:, ct * 257: ct * 257 + 256]),
                    waits=[txt, t_zero] + kw, inc="POOL" if ct == NCT - 1 else None)
                kw = []
            tmix = {}
            for name, buf, poff in (("k", kxb, 0), ("v", vxb, 8), ("r", rxb, 16)):
                tm = None
                mw = take({"k": "kxb", "v": "vxq", "r": "rxq"}[name])
                for ct in range(NCT):
                    tm = E("vector", lambda e, ct=ct, buf=buf, poff=poff: e.scalar_tensor_tensor(
                        ct_s(buf, ct), ct_s(Kf, ct), prm[:, poff + ct: poff + ct + 1],
                        xt[:, ct * 257: ct * 257 + 256], ALU.mult, ALU.add),
                        waits=[tdm, t_prm] + mw,
                        inc="DVE" if ct == NCT - 1 else None)
                    mw = []
                tmix[name] = tm
            tcar = None
            for ct in range(NCT):
                tcar = E("vector", lambda e, ct=ct: e.tensor_copy(
                    out=xtc[:, ct:ct + 1], in_=xt[:, ct * 257 + 256: ct * 257 + 257]),
                    waits=[txt], inc="DVE" if ct == NCT - 1 else None)
            tcol0 = None
            for ct in range(NCT):
                tcol0 = E("vector", lambda e, ct=ct: e.tensor_copy(
                    out=xt[:, ct * 257: ct * 257 + 1], in_=xtc[:, ct:ct + 1]),
                    waits=[tcar, tmix["k"], tmix["v"], tmix["r"]],
                    inc="DVE" if ct == NCT - 1 else None)
            use("xt", tcol0)
            evac_t = {}
            for name, src, m, tag in (("k", kxb, 0, "K"), ("v", vxb, 1, "V"),
                                      ("r", rxb, 2, "R")):
                ew = {"K": take("Kf2"), "V": take("Vb"), "R": take("Rb")}[tag]
                for ot in range(NCT):
                    pi = 2 + ot % 4
                    tmm = None
                    for ic in range(NCT):
                        tmm = E("tensor", lambda e, ot=ot, ic=ic, m=m, src=src, pi=pi: e.matmul(
                            PB[pi][:, 0:SUB],
                            attnW[m][:, (ot * 8 + ic) * 128:(ot * 8 + ic + 1) * 128],
                            ct_s(src, ic), start=(ic == 0), stop=(ic == NCT - 1)),
                            waits=[tmix[name], attnw_tick[m]] + (ptake(pi) if ic == 0 else []),
                            inc="PE" if ic == NCT - 1 else None)
                    if tag == "K":
                        tev = E("scalar", lambda e, ot=ot, pi=pi: e.copy(
                            out=ct_s(Kf, ot), in_=PB[pi][:, 0:SUB]),
                            waits=[tmm, tmix["r"]] + ew, inc="ACT")
                    elif tag == "V":
                        tev = E("scalar", lambda e, ot=ot, pi=pi: e.copy(
                            out=ct_s(Vb, ot), in_=PB[pi][:, 0:SUB]),
                            waits=[tmm] + ew, inc="ACT")
                    else:
                        tev = E("scalar", lambda e, ot=ot, pi=pi: e.activation(
                            out=ct_s(Rb, ot), in_=PB[pi][:, 0:SUB], func=AF.Sigmoid),
                            waits=[tmm] + ew, inc="ACT")
                    ew = []
                    puse(pi, tev)
                    evac_t[(tag, ot)] = tev
                    if ot == NCT - 1:
                        use({"K": "kxb", "V": "vxq", "R": "rxq"}[tag], tmm)
            tscan = {}
            for ct in range(NCT):
                ts = E("vector", lambda e, ct=ct: e.tensor_tensor_scan(
                    ct_s(Sf, ct), ct_s(decay_b, ct), ct_s(Kf, ct),
                    carryS[:, ct:ct + 1], ALU.mult, ALU.add),
                    waits=[evac_t[("K", ct)], t_decb, txt], inc="DVE")
                tscan[ct] = E("vector", lambda e, ct=ct: e.tensor_copy(
                    out=carryS[:, ct:ct + 1],
                    in_=Sf[:, ct * SUB + SUB - 1: ct * SUB + SUB]),
                    waits=[ts], inc="DVE")
            trk = {}
            rw = take("vxq")
            tp1 = None
            for ct in range(NCT):
                tp1 = E("vector", lambda e, ct=ct: e.scalar_tensor_tensor(
                    ct_s(Sf, ct), ct_s(Sf, ct), prm[:, 32 + ct:33 + ct],
                    ct_s(Kf, ct), ALU.mult, ALU.add),
                    waits=[tscan[ct], t_prm],
                    inc="DVE" if ct == NCT - 1 else None)
            tp2 = None
            for ct in range(NCT):
                tp2 = E("vector", lambda e, ct=ct: e.tensor_mul(
                    ct_s(Sf, ct), ct_s(Sf, ct), ct_s(Vb, ct)),
                    waits=[evac_t[("V", ct)], tp1],
                    inc="DVE" if ct == NCT - 1 else None)
            for ct in range(NCT):
                t3 = E("vector", lambda e, ct=ct: e.tensor_mul(
                    ct_s(rkb, ct), ct_s(Sf, ct), ct_s(Rb, ct)),
                    waits=[evac_t[("R", ct)], tp2] + rw, inc="DVE")
                rw = []
                use("Kf2", t3)
                use("Vb", t3)
                use("Rb", t3)
                trk[ct] = t3
            tatt = {}
            for ot in range(NCT):
                pi = 2 + ot % 4
                tmm = None
                for ic in range(NCT):
                    tmm = E("tensor", lambda e, ot=ot, ic=ic, pi=pi: e.matmul(
                        PB[pi][:, 0:SUB],
                        attnW[3][:, (ot * 8 + ic) * 128:(ot * 8 + ic + 1) * 128],
                        ct_s(rkb, ic), start=(ic == 0), stop=(ic == NCT - 1)),
                        waits=[trk[ic], attnw_tick[3]] + (ptake(pi) if ic == 0 else []),
                        inc="PE" if ic == NCT - 1 else None)
                use("vxq", tmm)
                ta = E("vector", lambda e, ot=ot, pi=pi: e.tensor_add(
                    ct_s(ln0, ot), ct_s(ln0, ot), PB[pi][:, 0:SUB]),
                    waits=[tmm, tl0], inc="DVE")
                puse(pi, ta)
                use("ln0", ta)
                tatt[ot] = ta
            tc2 = None
            cw = take("xbf")
            for ct in range(NCT):
                tc2 = E("scalar", lambda e, ct=ct: e.copy(
                    out=ct_s(xbf, ct), in_=ct_s(ln0, ct)),
                    waits=[tatt[ct]] + cw, inc="ACT" if ct == NCT - 1 else None)
                cw = []
            ts2 = None
            sw = take("xsq")
            for ct in range(NCT):
                ts2 = E("scalar", lambda e, ct=ct: e.activation(
                    out=ct_s(xsq, ct), in_=ct_s(xbf, ct), func=AF.Square),
                    waits=sw, inc="ACT" if ct == NCT - 1 else None)
                sw = []
            t_mu2, t_var2, t_rstd2, tmmA, tmmB = ln_stats(xbf, xsq, s_mu2, s_rstd2, tc2, ts2)
            use("xbf", tmmA)
            use("rxq", tmmB)
            tx1 = {}
            tq1 = None
            for ct in range(NCT):
                tq1 = E("gpsimd", lambda e, ct=ct: e.tensor_sub(
                    xt[:, ct * 257 + 1: ct * 257 + 257], ct_s(ln0, ct), STT(s_mu2)),
                    waits=[t_mu2, tatt[ct], tcol0],
                    inc="POOL" if ct == NCT - 1 else None)
            tq2 = None
            for ct in range(NCT):
                tq2 = E("gpsimd", lambda e, ct=ct: e.tensor_mul(
                    xt[:, ct * 257 + 1: ct * 257 + 257],
                    xt[:, ct * 257 + 1: ct * 257 + 257], STT(s_rstd2)),
                    waits=[t_rstd2, tq1],
                    inc="POOL" if ct == NCT - 1 else None)
            for ct in range(NCT):
                tx1[ct] = E("gpsimd", lambda e, ct=ct: e.tensor_add(
                    ct_s(xfm, ct), xt[:, ct * 257 + 1: ct * 257 + 257],
                    ct_s(xfm, ct)),
                    waits=[tl0, tq2], inc="POOL")
                use("ln0", tx1[ct])
                use("xt", tx1[ct])
            if tap is None:
                for ct in range(NCT):
                    tst = kb.emit("scalar", lambda e, ct=ct, t0=t0: e.dma_start(
                        out=x1_dram[ct][:, t0:t0 + SUB], in_=ct_s(xfm, ct)),
                        waits=[tx1[ct]], inc="X1S", amt=16)
                    use("xfm", tst)
                prev["x1st"] = tst
            else:
                for ct in range(NCT):
                    if tap == "xt":
                        tw, view = [txt, tx1[ct]], xt[:, ct * 257 + 1: ct * 257 + 257]
                    elif tap == "K":
                        tw, view = [evac_t[("K", ct)], tscan[ct], tx1[ct]], ct_s(Kf, ct)
                    elif tap == "S":
                        tw, view = [trk[ct], tx1[ct]], ct_s(Sf, ct)
                    elif tap == "att":
                        tw, view = [tatt[ct], tx1[ct]], ct_s(ln0, ct)
                    else:
                        tw, view = [tx1[ct]], ct_s(xfm, ct)
                    tst = kb.store(x1_dram[ct][:, t0:t0 + SUB], view, waits=tw)
                    use("xfm", tst)
                    use("ln0", tst)
                    use("xt", tst)
                    use("Kf", tst)
                    use("Sf", tst)
            tc3 = None
            cw = take("xbf")
            for ct in range(NCT):
                tc3 = E("scalar", lambda e, ct=ct: e.copy(
                    out=ct_s(xbf, ct), in_=ct_s(xfm, ct)),
                    waits=[tx1[ct]] + cw, inc="ACT" if ct == NCT - 1 else None)
                cw = []
            ts3 = None
            sw = take("xsq")
            for ct in range(NCT):
                ts3 = E("scalar", lambda e, ct=ct: e.activation(
                    out=ct_s(xsq, ct), in_=ct_s(xbf, ct), func=AF.Square),
                    waits=sw, inc="ACT" if ct == NCT - 1 else None)
                sw = []
            t_mul, t_varl, t_rstdl, tmmA, tmmB = ln_stats(xbf, xsq, s_mu2, s_rstd2, tc3, ts3)
            use("xbf", tmmA)
            use("rxq", tmmB)
            off = (j % SUBS) * SUB
            tln1 = None
            kw = take("Kf") + take("Kf2")
            for ct in range(NCT):
                tln1 = E("vector", lambda e, ct=ct: e.tensor_sub(
                    ct_s(Kf, ct), ct_s(xfm, ct), STT(s_mu2)),
                    waits=[t_mul, tx1[ct]] + kw,
                    inc="DVE" if ct == NCT - 1 else None)
                kw = []
            tlsub = tln1
            lww = take("ln1w") if (j % SUBS) == 0 else []
            for ct in range(NCT):
                tln1 = E("vector", lambda e, ct=ct, off=off: e.tensor_mul(
                    ln1[:, ct * SC + off: ct * SC + off + SUB],
                    ct_s(Kf, ct), STT(s_rstd2)),
                    waits=[t_rstdl, tlsub] + lww, inc="DVE" if ct == NCT - 1 else None)
                lww = []
            use("Kf", tln1)
            use("xfm", tln1)
            prev["ln1"] = tln1
            if (j % SUBS) == SUBS - 1:
                tls = None
                for ct in range(NCT):
                    tls = kb.emit("scalar", lambda e, ct=ct, sc=sc: e.dma_start(
                        out=ln1_dram[ct][:, sc * SC:(sc + 1) * SC],
                        in_=ln1[:, ct * SC:(ct + 1) * SC]),
                        waits=[tln1], inc="L1S", amt=16)
                    use("ln1w", tls)
                prev["l1st"] = tls

        def load_slab(m, ot, dstview, warkey):
            return kb.load(dstview, mt[m][ot], waits=[mat_ready[m]] + take(warkey))

        def moe_sc(sc):
            tl1 = None
            l1w = take("ln1buf")
            for ct in range(NCT):
                tl1 = kb.load(ln1[:, ct * SC:(ct + 1) * SC],
                              ln1_dram[ct][:, sc * SC:(sc + 1) * SC],
                              waits=[prev.get("l1st")] + l1w)
                l1w = []
            prev["ln1"] = tl1
            if tap == "moe" and sc == 0:
                kb.gdma(taps["t_mt7"], mt[7], waits=[mat_ready[7]])
                kb.gdma(taps["t_mt9"], mt[9], waits=[mat_ready[9]])
            if tap == "moe":
                for ct in range(NCT):
                    kb.gdma(taps["t_ln1"][ct][:, sc * SC:(sc + 1) * SC],
                            ln1[:, ct * SC:(ct + 1) * SC], waits=[prev["ln1"]])
            t_cb = {}
            for s in range(NSL):
                soff = s * SL
                tg = None
                for ic in range(NCT):
                    tg = E("tensor", lambda e, ic=ic, soff=soff: e.matmul(
                        PB[6][0:NE, 0:SL], gwT[:, ic * NE:(ic + 1) * NE],
                        ln1[:, ic * SC + soff: ic * SC + soff + SL],
                        start=(ic == 0), stop=(ic == NCT - 1)),
                        waits=[prev["ln1"], t_gwT] + (ptake(6) if ic == 0 else []),
                        inc="PE" if ic == NCT - 1 else None)
                tsc = E("scalar", lambda e: e.activation(
                    out=sco[0:NE, 0:SL], in_=PB[6][0:NE, 0:SL], func=AF.Sigmoid),
                    waits=[tg] + take("sco"), inc="ACT")
                puse(6, tsc)
                if tap == "moe":
                    ts_t = kb.store(taps["t_sco"][:, sc * SC + soff: sc * SC + soff + SL],
                                    sco[0:NE, 0:SL], waits=[tsc])
                    use("sco", ts_t)
                tse = E("vector", lambda e: e.tensor_scalar_add(
                    sel[0:NE, 0:SL], sco[0:NE, 0:SL], ebias[0:NE, 0:1]),
                    waits=[tsc, t_eb] + take("sel"), inc="DVE")
                tevs = []
                for tb in range(4):
                    tt1 = E("tensor", lambda e, tb=tb: e.transpose(
                        PB[7][0:128, 0:NE], sco[0:NE, tb * 128:(tb + 1) * 128],
                        idn[0:NE, 0:NE]),
                        waits=[tsc, t_idn] + ptake(7), inc="PE")
                    te1 = E("scalar", lambda e, tb=tb: e.copy(
                        out=r_tm[:, tb * NE:(tb + 1) * NE], in_=PB[7][0:128, 0:NE]),
                        waits=[tt1] + (take("r_tm") if tb == 0 else []), inc="ACT")
                    puse(7, te1)
                    tt2 = E("tensor", lambda e, tb=tb: e.transpose(
                        PB[7][0:128, 0:NE], sel[0:NE, tb * 128:(tb + 1) * 128],
                        idn[0:NE, 0:NE]),
                        waits=[tse, t_idn] + ptake(7), inc="PE")
                    te2 = E("scalar", lambda e, tb=tb: e.copy(
                        out=l_tm[:, tb * NE:(tb + 1) * NE], in_=PB[7][0:128, 0:NE]),
                        waits=[tt2] + (take("l_tm") if tb == 0 else []), inc="ACT")
                    puse(7, te2)
                    tevs.append((te1, te2))
                use("sco", tevs[-1][0])
                use("sel", tevs[-1][1])
                tmsk = None
                mw = take("msk")
                rw0 = take("rnk")
                te2 = tevs[3][1]
                tround = None
                for tb in range(4):
                    tround = E("vector", lambda e, tb=tb: e.tensor_single_scalar(
                        rnk[:, tb * NE:(tb + 1) * NE],
                        l_tm[:, tb * NE:(tb + 1) * NE],
                        l_tm[:, tb * NE: tb * NE + 1], ALU.is_lt),
                        waits=[te2, tevs[tb][1]] + rw0,
                        inc="DVE" if tb == 3 else None)
                    rw0 = []
                for jj in range(1, NE):
                    for tb in range(4):
                        tround = E("vector", lambda e, tb=tb, jj=jj: e.scalar_tensor_tensor(
                            rnk[:, tb * NE:(tb + 1) * NE],
                            l_tm[:, tb * NE:(tb + 1) * NE],
                            l_tm[:, tb * NE + jj: tb * NE + jj + 1],
                            rnk[:, tb * NE:(tb + 1) * NE], ALU.is_lt, ALU.add),
                            waits=[tround] if tb == 0 else None,
                            inc="DVE" if tb == 3 else None)
                for jj in range(NE - 1):
                    for tb in range(4):
                        tround = E("vector", lambda e, tb=tb, jj=jj: e.scalar_tensor_tensor(
                            rnk[:, tb * NE + jj + 1:(tb + 1) * NE],
                            l_tm[:, tb * NE + jj + 1:(tb + 1) * NE],
                            l_tm[:, tb * NE + jj: tb * NE + jj + 1],
                            rnk[:, tb * NE + jj + 1:(tb + 1) * NE],
                            ALU.is_equal, ALU.add),
                            waits=[tround] if tb == 0 else None,
                            inc="DVE" if tb == 3 else None)
                for tb in range(4):
                    tmsk = E("vector", lambda e, tb=tb: e.tensor_single_scalar(
                        msk[:, tb * NE:(tb + 1) * NE],
                        rnk[:, tb * NE:(tb + 1) * NE], float(TOPK), ALU.is_lt),
                        waits=mw + [tround], inc="DVE" if tb == 3 else None)
                    mw = []
                use("l_tm", tmsk)
                use("rnk", tmsk)
                if tap == "moe":
                    tm_t = kb.store(
                        taps["t_msk"][:, ((sc * SC + soff) // 128) * NE:
                                      ((sc * SC + soff) // 128 + 4) * NE],
                        msk[:], waits=[tmsk])
                    use("msk", tm_t)
                tcu = None
                for tb in range(4):
                    tcu = E("vector", lambda e, tb=tb: e.tensor_mul(
                        r_tm[:, tb * NE:(tb + 1) * NE],
                        r_tm[:, tb * NE:(tb + 1) * NE],
                        msk[:, tb * NE:(tb + 1) * NE]),
                        waits=[tmsk], inc="DVE" if tb == 3 else None)
                use("msk", tcu)
                tden = None
                dw = take("den")
                for tb in range(4):
                    tden = E("vector", lambda e, tb=tb: e.reduce_sum(
                        den[:, tb:tb + 1], r_tm[:, tb * NE:(tb + 1) * NE], axis=AX.X),
                        waits=[tcu] + dw, inc="DVE" if tb == 3 else None)
                    dw = []
                trcp = E("vector", lambda e: e.reciprocal(den[:, 4:8], den[:, 0:4]),
                         waits=[tden], inc="DVE")
                tcmb = None
                for tb in range(4):
                    tcmb = E("vector", lambda e, tb=tb: e.tensor_scalar(
                        r_tm[:, tb * NE:(tb + 1) * NE],
                        r_tm[:, tb * NE:(tb + 1) * NE],
                        den[:, 4 + tb:5 + tb], ROUTE_SCALE, ALU.mult, ALU.mult),
                        waits=[trcp], inc="DVE" if tb == 3 else None)
                use("den", tcmb)
                tcf = None
                fw = take("cfm")
                for tb in range(4):
                    tt = E("tensor", lambda e, tb=tb: e.transpose(
                        PB[7][0:NE, 0:128], r_tm[:, tb * NE:(tb + 1) * NE], idn[:]),
                        waits=[tcmb, t_idn] + ptake(7), inc="PE")
                    use("r_tm", tt)
                    tcf = E("scalar", lambda e, tb=tb: e.copy(
                        out=cfm[0:NE, tb * 128:(tb + 1) * 128], in_=PB[7][0:NE, 0:128]),
                        waits=[tt] + fw, inc="ACT")
                    fw = []
                    puse(7, tcf)
                if tap == "moe":
                    tc_t = kb.store(taps["t_cmb"][:, sc * SC + soff: sc * SC + soff + SL],
                                    cfm[0:NE, 0:SL], waits=[tcf])
                    use("cfm", tc_t)
                g_idx = (sc * NSL + s) % 2
                tstc = kb.store(cdram[g_idx], cfm[0:NE, 0:SL],
                                waits=[tcf] + take(f"cdram{g_idx}"))
                use("cfm", tstc)
                t_cb[s] = (g_idx, tstc)

            tz = None
            zw = take("y_acc") + take("y_accr")
            for dt in range(NCT):
                tz = E("gpsimd", lambda e, dt=dt: e.memset(
                    y_acc[:, dt * SC:(dt + 1) * SC], 0.0),
                    waits=zw, inc="POOL" if dt == NCT - 1 else None)
                zw = []
            t_yz = tz

            t_hp = {}

            def up_expert(e_id, pp, m_g, m_u, is_shared):
                tcbl = {}
                if not is_shared:
                    for s in range(NSL):
                        g_idx, tstc = t_cb[s]
                        tcbl[s] = kb.emit("sync", lambda e, pp=pp, s=s, g_idx=g_idx, e_id=e_id: e.dma_start(
                            out=cb[pp][s],
                            in_=cdram[g_idx][e_id:e_id + 1, :].broadcast_to([128, SL])),
                            waits=[tstc] + take(f"cb{pp}{s}"), inc="CBS", amt=16)
                        use(f"cdram{g_idx}", tcbl[s])
                thp = None
                for it in range(NCT):
                    bs = it % 2
                    tlg = load_slab(m_g, it, upg[bs][:], f"upg{bs}")
                    tlu = load_slab(m_u, it, upu[bs][:], f"upu{bs}")
                    gb = [0 + 2 * bs, 1 + 2 * bs]
                    ub = [4 + 2 * bs, 5 + 2 * bs]
                    tgm = {}
                    tum = {}
                    for ic in range(NCT):
                        for s in range(NSL):
                            tgm[s] = E("tensor", lambda e2, ic=ic, s=s, bs=bs, gi=gb[s]: e2.matmul(
                                PB[gi][:, 0:SL], upg[bs][:, ic * 128:(ic + 1) * 128],
                                ln1[:, ic * SC + s * SL: ic * SC + (s + 1) * SL],
                                start=(ic == 0), stop=(ic == NCT - 1)),
                                waits=[tlg, prev["ln1"]] + (ptake(gb[s]) if ic == 0 else []),
                                inc="PE" if ic == NCT - 1 else None)
                        for s in range(NSL):
                            tum[s] = E("tensor", lambda e2, ic=ic, s=s, bs=bs, ui=ub[s]: e2.matmul(
                                PB[ui][:, 0:SL], upu[bs][:, ic * 128:(ic + 1) * 128],
                                ln1[:, ic * SC + s * SL: ic * SC + (s + 1) * SL],
                                start=(ic == 0), stop=(ic == NCT - 1)),
                                waits=[tlu] + (ptake(ub[s]) if ic == 0 else []),
                                inc="PE" if ic == NCT - 1 else None)
                    use(f"upg{bs}", tgm[NSL - 1])
                    use(f"upu{bs}", tum[NSL - 1])
                    for s in range(NSL):
                        tsg = E("scalar", lambda e2, s=s, gi=gb[s]: e2.activation(
                            out=sgb[s][:], in_=PB[gi][:, 0:SL], func=AF.Silu),
                            waits=[tgm[s]] + take(f"sgb{s}"), inc="ACT")
                        puse(gb[s], tsg)
                        if is_shared:
                            thu = E("vector", lambda e2, s=s, ui=ub[s]: e2.tensor_copy(
                                out=hub[s][:], in_=PB[ui][:, 0:SL]),
                                waits=[tum[s]] + take(f"hub{s}"), inc="DVE")
                        else:
                            thu = E("vector", lambda e2, s=s, ui=ub[s], pp=pp: e2.tensor_mul(
                                hub[s][:], PB[ui][:, 0:SL], cb[pp][s]),
                                waits=[tum[s], tcbl[s]] + take(f"hub{s}"), inc="DVE")
                            use(f"cb{pp}{s}", thu)
                        puse(ub[s], thu)
                        thp = E("gpsimd", lambda e2, s=s, it=it, pp=pp: e2.tensor_mul(
                            hp[pp][:, hp_off(it, s): hp_off(it, s) + SL],
                            sgb[s][:], hub[s][:]),
                            waits=[tsg, thu] + (take(f"hp{pp}") if (it == 0 and s == 0) else []),
                            inc="POOL")
                        use(f"sgb{s}", thp)
                        use(f"hub{s}", thp)
                        t_hp[(pp, it, s)] = thp
                return thp

            def down_group(members, m_ds, tap_this=False):
                last_pe = None
                for dt in range(NCT):
                    ds = dt % 2
                    db = [0 + 2 * ds, 1 + 2 * ds]
                    tmm = {}
                    first = True
                    for pi_, (pp, m_d) in enumerate(zip(members, m_ds)):
                        tld = load_slab(m_d, dt, wdb[ds][:], f"wdb{ds}")
                        is_last_m = (pi_ == len(members) - 1)
                        for ic in range(NCT):
                            for s in range(NSL):
                                w = [tld, t_hp[(pp, ic, s)]]
                                if first and ic == 0:
                                    w += ptake(db[s])
                                tmm[s] = E("tensor", lambda e2, ic=ic, s=s, ds=ds, pp=pp, di=db[s], st=(first and ic == 0), sp=(is_last_m and ic == NCT - 1): e2.matmul(
                                    PB[di][:, 0:SL], wdb[ds][:, ic * 128:(ic + 1) * 128],
                                    hp[pp][:, hp_off(ic, s): hp_off(ic, s) + SL],
                                    start=st, stop=sp),
                                    waits=w,
                                    inc="PE" if ic == NCT - 1 else None)
                        first = False
                        use(f"wdb{ds}", tmm[NSL - 1])
                        last_pe = tmm[NSL - 1]
                        if False and tap_this and dt == 0 and pi_ == 0:
                            tw = kb.gdma(taps["t_wdb"], wdb[ds][:], waits=[tld])
                            use(f"wdb{ds}", tw)
                    if False and tap_this and dt == 0:
                        tdp = E("scalar", lambda e2, di=db[0]: e2.copy(
                            out=taps and fin[:, 0:SL], in_=PB[di][:, 0:SL]),
                            waits=[tmm[0]] + take("fin"), inc="ACT")
                        puse(db[0], tdp)
                        tps = kb.store(taps["t_dps"], fin[:, 0:SL], waits=[tdp])
                        use("fin", tps)
                    for s in range(NSL):
                        ty = E("vector", lambda e2, dt=dt, s=s, di=db[s]: e2.tensor_add(
                            y_acc[:, dt * SC + s * SL: dt * SC + (s + 1) * SL],
                            y_acc[:, dt * SC + s * SL: dt * SC + (s + 1) * SL],
                            PB[di][:, 0:SL]),
                            waits=[tmm[s], t_yz], inc="DVE")
                        puse(db[s], ty)
                        prev["y_done"] = ty
                for pp in members:
                    use(f"hp{pp}", last_pe)

            pairs = range(NE // 2) if MOE_PAIRS is None else MOE_PAIRS
            for p in pairs:
                e0, e1 = 2 * p, 2 * p + 1
                th0 = up_expert(e0, 0, 7 + 3 * e0, 7 + 3 * e0 + 1, False)
                th1 = up_expert(e1, 1, 7 + 3 * e1, 7 + 3 * e1 + 1, False)
                tap_p = (tap == "moe" and sc == 0 and p == (list(pairs)[0] if MOE_PAIRS else 0))
                if tap_p:
                    for pp, th in ((0, th0), (1, th1)):
                        tk = kb.gdma(taps["t_hp"][pp], hp[pp], waits=[th])
                        use(f"hp{pp}", tk)
                down_group([0, 1], [7 + 3 * e0 + 2, 7 + 3 * e1 + 2], tap_this=tap_p)

            if MOE_SHARED:
                if sc == 0 and 4 not in mat_ready:
                    for m_s in (4, 5, 6):
                        phase0(m_s)
                up_expert(None, 0, 4, 5, True)
                down_group([0], [6])

            if tap == "moe":
                for ct in range(NCT):
                    ty_t = kb.store(taps["t_ymoe"][ct][:, sc * SC:(sc + 1) * SC],
                                    y_acc[:, ct * SC:(ct + 1) * SC],
                                    waits=[prev["y_done"]])
                    use("y_accr", ty_t)
            for s in range(NSL):
                soff = s * SL
                tgl = sc * SC + soff
                tx1l = {}
                for ct in range(NCT):
                    tx1l[ct] = kb.load(
                        x1r[:, ct * SL:(ct + 1) * SL],
                        x1_dram[ct][:, tgl:tgl + SL],
                        waits=take(f"x1r{ct}") + [prev.get("x1st")])
                tcy = None
                cw = take("fin")
                for ct in range(NCT):
                    tcy = E("scalar", lambda e2, ct=ct, soff=soff: e2.copy(
                        out=ybf[:, ct * SL:(ct + 1) * SL],
                        in_=y_acc[:, ct * SC + soff: ct * SC + soff + SL]),
                        waits=[prev["y_done"]] + cw, inc="ACT" if ct == NCT - 1 else None)
                    cw = []
                tsy = None
                for ct in range(NCT):
                    tsy = E("scalar", lambda e2, ct=ct: e2.activation(
                        out=ysq[:, ct * SL:(ct + 1) * SL],
                        in_=ybf[:, ct * SL:(ct + 1) * SL], func=AF.Square),
                        inc="ACT" if ct == NCT - 1 else None)
                tmmA = None
                for ct in range(NCT):
                    tmmA = E("tensor", lambda e2, ct=ct: e2.matmul(
                        PB[6][:, 0:SL], ones_bf[:], ybf[:, ct * SL:(ct + 1) * SL],
                        start=(ct == 0), stop=(ct == NCT - 1)),
                        waits=[tcy] + (ptake(6) if ct == 0 else []),
                        inc="PE" if ct == NCT - 1 else None)
                tmmB = None
                for ct in range(NCT):
                    tmmB = E("tensor", lambda e2, ct=ct: e2.matmul(
                        PB[7][:, 0:SL], ones_bf[:], ysq[:, ct * SL:(ct + 1) * SL],
                        start=(ct == 0), stop=(ct == NCT - 1)),
                        waits=[tsy] + (ptake(7) if ct == 0 else []),
                        inc="PE" if ct == NCT - 1 else None)
                mu3 = stats[:, s_mu * SUB: s_mu * SUB + SL]
                e23 = stats[:, s_var * SUB: s_var * SUB + SL]
                rstd3 = stats[:, s_rstd * SUB: s_rstd * SUB + SL]
                sp3 = stats[:, s_vi * SUB: s_vi * SUB + SL]
                sp4 = stats[:, s_mu2 * SUB: s_mu2 * SUB + SL]
                t_mu3 = E("scalar", lambda e2: e2.mul(mu3, PB[6][:, 0:SL], np.float32(1.0 / C)),
                          waits=[tmmA], inc="ACT")
                puse(6, t_mu3)
                t_e23 = E("scalar", lambda e2: e2.mul(e23, PB[7][:, 0:SL], np.float32(1.0 / C)),
                          waits=[tmmB], inc="ACT")
                puse(7, t_e23)
                t_m3 = E("gpsimd", lambda e2: e2.tensor_mul(sp3, mu3, mu3),
                         waits=[t_mu3], inc="POOL")
                t_v3 = E("vector", lambda e2: e2.tensor_sub(e23, e23, sp3),
                         waits=[t_e23, t_m3], inc="DVE")
                t_sd3 = E("scalar", lambda e2: e2.activation(
                    out=sp4, in_=e23, func=AF.Sqrt, bias=np.float32(LN_EPS)),
                    waits=[t_v3], inc="ACT")
                t_r3 = E("vector", lambda e2: e2.reciprocal(rstd3, sp4),
                         waits=[t_sd3], inc="DVE")
                finf = fin
                tx2 = {}
                fw = [tmmA, tmmB]
                tf1 = None
                for ct in range(NCT):
                    tf1 = E("gpsimd", lambda e2, ct=ct, soff=soff: e2.tensor_sub(
                        finf[:, ct * SL:(ct + 1) * SL],
                        y_acc[:, ct * SC + soff: ct * SC + soff + SL], mu3),
                        waits=[t_mu3] + fw,
                        inc="POOL" if ct == NCT - 1 else None)
                    fw = []
                tf2 = None
                for ct in range(NCT):
                    tf2 = E("gpsimd", lambda e2, ct=ct: e2.tensor_mul(
                        finf[:, ct * SL:(ct + 1) * SL],
                        finf[:, ct * SL:(ct + 1) * SL], rstd3),
                        waits=[t_r3, tf1],
                        inc="POOL" if ct == NCT - 1 else None)
                for ct in range(NCT):
                    tx2[ct] = E("gpsimd", lambda e2, ct=ct: e2.tensor_add(
                        x1r[:, ct * SL:(ct + 1) * SL],
                        x1r[:, ct * SL:(ct + 1) * SL],
                        finf[:, ct * SL:(ct + 1) * SL]),
                        waits=[tx1l[ct], tf2], inc="POOL")
                    use("fin", tx2[ct])
                    use("y_accr", tx2[ct])
                for tb in range(4):
                    bs = tb % 2
                    tevl = None
                    for ct in range(NCT):
                        pi = 6 + (ct % 2)
                        ttr = E("tensor", lambda e2, tb=tb, ct=ct, pi=pi: e2.transpose(
                            PB[pi][:, 0:128],
                            x1r[:, ct * SL + tb * 128: ct * SL + (tb + 1) * 128],
                            idn[:]),
                            waits=[tx2[ct], t_idn] + ptake(pi), inc="PE")
                        use(f"x1r{ct}", ttr)
                        tevl = E("scalar", lambda e2, ct=ct, pi=pi, bs=bs: e2.copy(
                            out=x2tm[bs][:, ct * 128:(ct + 1) * 128],
                            in_=PB[pi][:, 0:128]),
                            waits=[ttr] + (take(f"x2tm{bs}") if ct == 0 else []),
                            inc="ACT")
                        puse(pi, tevl)
                    tst = kb.store(y_d[tgl + tb * 128: tgl + (tb + 1) * 128, :],
                                   x2tm[bs][:], waits=[tevl])
                    use(f"x2tm{bs}", tst)
            use("ln1buf", prev["y_done"])
            use("y_accr", prev["y_done"])

        p0_queue = [7 + 3 * e + j for e in range(2, NE) for j in range(3)] + [4, 5, 6]
        nsub_total = NSC * SUBS
        for sc in range(NSC):
            kb.set_epoch(sc)
            for jj in range(SUBS):
                attn_sub(sc, sc * SUBS + jj)
                done = sc * SUBS + jj + 1
                want = (len(p0_queue) + 45 - 45) and int(45 * done / nsub_total)
                while 45 - len(p0_queue) < want and p0_queue:
                    phase0(p0_queue.pop(0))
        if stage >= 2:
            kb.barrier()
            for sc in range(NSC):
                kb.set_epoch(sc)
                moe_sc(sc)
        kb.barrier()

        allst = [(s, kb.cnt[s]) for s in kb.sem
                 if (s.startswith("ST") or s == "GST") and kb.cnt[s] > 0]
        if stage == 1:
            tlast = kb.gdma(_flat3(y_d), x1_dram, waits=allst)
            kb.emit("gpsimd", lambda e: e.nop(), waits=[tlast])
        else:
            kb.emit("scalar", lambda e: e.nop(), waits=allst)

        kb.finalize(block)
    return nc


def _flat3(y_ap):
    return y_ap.rearrange("a c -> (a c)").rearrange(
        "(ct p t) -> ct p t", ct=NCT, p=128, t=T)


def kernel(**inputs):
    nc = build_program(stage=2)
    xs = np.ascontiguousarray(inputs["L0_attn_x"])
    names = {
        "time_mix_k": "L0_attn_time_mix_k", "time_mix_v": "L0_attn_time_mix_v",
        "time_mix_r": "L0_attn_time_mix_r", "time_decay": "L0_attn_time_decay",
        "time_first": "L0_attn_time_first", "Wk": "L0_attn_Wk", "Wv": "L0_attn_Wv",
        "Wr": "L0_attn_Wr", "Wo": "L0_attn_Wo", "ln_w": "L0_attn_ln_w",
        "ln_b": "L0_attn_ln_b", "gate_w": "L0_ffn_gate_w",
        "expert_bias": "L0_ffn_expert_bias", "w_gate": "L0_ffn_w_gate",
        "w_up": "L0_ffn_w_up", "w_down": "L0_ffn_w_down",
        "shared_gate_w": "L0_ffn_shared_gate_w", "shared_up_w": "L0_ffn_shared_up_w",
        "shared_down_w": "L0_ffn_shared_down_w",
    }
    shared = {k: np.ascontiguousarray(np.asarray(inputs[v], dtype=np.float32))
              for k, v in names.items()}
    in_maps = [dict(shared, x=np.ascontiguousarray(xs[b])) for b in range(8)]
    res = run_bass_kernel_spmd(nc, in_maps, core_ids=list(range(8)))
    return np.stack([np.asarray(r["y"]) for r in res.results], axis=0)
